# version 01
# baseline (speedup 1.0000x reference)
import numpy as np
import ml_dtypes

import concourse.bass as bass
import concourse.mybir as mybir
from concourse.bass_utils import run_bass_kernel_spmd

F32 = mybir.dt.float32
BF16 = mybir.dt.bfloat16
AF = mybir.ActivationFunctionType
ALU = mybir.AluOpType
AX = mybir.AxisListType

NCORE = 8
NTOK = 2048
H = 4096
KC = 32
S = 1024
SCN = 8
FF = 9984
FCN = 78
NT = 4
EPS = 1e-5

ALL_PHASES = ("ln", "moe", "g", "b", "d", "c", "e")

SEMS = ("CLS", "CLG", "X0", "X1", "WA", "WB", "OL", "EL",
        "TS0", "TS1", "TS2", "TS3", "AS0", "AS1", "GT", "OS", "DBG", "GS",
        "MM", "EV", "AC")


class Prog:
    ENGINES = ("sync", "gpsimd", "tensor", "vector", "scalar")

    def __init__(self):
        self.items = {e: [] for e in self.ENGINES}
        self.cnt = {s: 0 for s in SEMS}

    AUTO = {"vector": "EV", "scalar": "AC"}

    def emit(self, eng, fn, wait=(), inc=()):
        inc = list(inc)
        wait = list(wait)
        auto = self.AUTO.get(eng)
        if auto is not None:
            wait.append((auto, self.cnt[auto]))
            if not any(s == auto for (s, n) in inc):
                inc.append((auto, 1))
        w = [(s, v) for (s, v) in wait if v > 0]
        for s, n in inc:
            self.cnt[s] += n
        self.items[eng].append((fn, w, list(inc)))
        return {s: self.cnt[s] for (s, n) in inc}

    def val(self, s):
        return self.cnt[s]


def _emitter(items, sems):
    def run(eng):
        last = {}
        for fn, waits, incs in items:
            for s, v in waits:
                if last.get(s, -1) < v:
                    eng.wait_ge(sems[s], v)
                    last[s] = v
            inst = fn(eng)
            for s, n in incs:
                inst = inst.then_inc(sems[s], n)
    return run


def build(phases=ALL_PHASES, debug=False):
    nc = bass.Bass("TRN2", target_bir_lowering=False)

    EI = dict(kind="ExternalInput")
    x_fm = nc.dram_tensor("x_fm", [H, NTOK], F32, **EI)
    upb = nc.dram_tensor("upb", [SCN, KC, 128, 128], F32, **EI)
    gab = nc.dram_tensor("gab", [SCN, KC, 128, 128], F32, **EI)
    w7b = nc.dram_tensor("w7b", [2, KC, 7, 4, 128, 128], F32, **EI)
    wgb = nc.dram_tensor("wgb", [FCN, KC, 128, 128], F32, **EI)
    wub = nc.dram_tensor("wub", [FCN, KC, 128, 128], F32, **EI)
    wdb = nc.dram_tensor("wdb", [3, KC, 26, 128, 128], F32, **EI)
    wbb = nc.dram_tensor("wbb", [KC, KC, 128, 128], F32, **EI)
    aab = nc.dram_tensor("aab", [KC, 128, 128], F32, **EI)
    btb = nc.dram_tensor("btb", [128, H], F32, **EI)
    w9b = nc.dram_tensor("w9b", [128, 288], F32, **EI)
    lpb = nc.dram_tensor("lpb", [128, 8], F32, **EI)
    bbb = nc.dram_tensor("bbb", [128, 32], F32, **EI)
    bthr = nc.dram_tensor("bthr", [1, 1], F32, **EI)
    dfac = nc.dram_tensor("dfac", [1, 6], F32, **EI)
    onesk = nc.dram_tensor("onesk", [128, 1], F32, **EI)
    ones1 = nc.dram_tensor("ones1", [1, 128], F32, **EI)
    identf = nc.dram_tensor("identf", [128, 128], F32, **EI)
    indw = nc.dram_tensor("indw", [8, 128], F32, **EI)
    cvals = nc.dram_tensor("cvals", [1, 4], F32, **EI)

    out_d = nc.dram_tensor("out", [KC, 128, NTOK], F32, kind="ExternalOutput")

    skind = "ExternalOutput" if debug else "Internal"
    tmp_d = nc.dram_tensor("tmp_d", [7, SCN, 128, NTOK], BF16, kind=skind)
    a_d = nc.dram_tensor("a_d", [FCN, 128, NTOK], BF16, kind=skind)
    g_d = nc.dram_tensor("g_d", [KC, 128, NTOK], F32, kind=skind)
    s_d = nc.dram_tensor("s_d", [KC, 128, NTOK], F32, kind=skind)
    if debug:
        xn_d = nc.dram_tensor("xn_d", [KC, 128, NTOK], BF16, kind="ExternalOutput")
        wh_d = nc.dram_tensor("wh_d", [128, NTOK], BF16, kind="ExternalOutput")
        h_d = nc.dram_tensor("h_d", [128, NTOK], F32, kind="ExternalOutput")
        gt_d = nc.dram_tensor("gt_d", [128, 144], F32, kind="ExternalOutput")
        wf_d = nc.dram_tensor("wf_d", [8, NTOK], BF16, kind="ExternalOutput")

    xnbuf = nc.alloc_sbuf_tensor("xnbuf", [128, 65536], BF16)
    wbufA = nc.alloc_sbuf_tensor("wbufA", [128, 8192], BF16)
    wbufB = nc.alloc_sbuf_tensor("wbufB", [128, 8192], BF16)
    stf = nc.alloc_sbuf_tensor("stf", [128, 2048], F32)
    bfst0 = nc.alloc_sbuf_tensor("bfst0", [128, 2048], BF16)
    bfst1 = nc.alloc_sbuf_tensor("bfst1", [128, 2048], BF16)
    scr = nc.alloc_sbuf_tensor("scr", [128, 6144], F32)
    id_bf = nc.alloc_sbuf_tensor("id_bf", [128, 128], BF16)
    id_f32 = nc.alloc_sbuf_tensor("id_f32", [128, 128], F32)
    ind_bf = nc.alloc_sbuf_tensor("ind_bf", [8, 128], BF16)
    onesk_bf = nc.alloc_sbuf_tensor("onesk_bf", [128, 1], BF16)
    ones1_f = nc.alloc_sbuf_tensor("ones1_f", [1, 128], F32)
    w9_bf = nc.alloc_sbuf_tensor("w9_bf", [128, 288], BF16)
    lp_sb = nc.alloc_sbuf_tensor("lp_sb", [128, 8], F32)
    lpneg = nc.alloc_sbuf_tensor("lpneg", [128, 8], F32)
    bb_sb = nc.alloc_sbuf_tensor("bb_sb", [128, 32], F32)
    bthr_sb = nc.alloc_sbuf_tensor("bthr_sb", [1, 1], F32)
    bthr_bc = nc.alloc_sbuf_tensor("bthr_bc", [128, 1], F32)
    dfac_sb = nc.alloc_sbuf_tensor("dfac_sb", [1, 6], F32)
    c_row = nc.alloc_sbuf_tensor("c_row", [1, 6], F32)
    c_bc = nc.alloc_sbuf_tensor("c_bc", [128, 6], F32)
    cv_sb = nc.alloc_sbuf_tensor("cv_sb", [1, 4], F32)

    xf32 = xnbuf.bitcast(F32)
    scr_bf = scr.bitcast(BF16)

    P0 = nc.alloc_psum_tensor("P0", [128, 2048], F32)
    P1 = nc.alloc_psum_tensor("P1", [128, 2048], F32)
    P0b = P0.bitcast(BF16)
    PS = [P0, P1]

    p = Prog()
    sem_handles = {}

    def xn(kc, t0, n):
        return xnbuf[:, kc * 2048 + t0: kc * 2048 + t0 + n]

    def cop(j, t0, n):
        return xnbuf[:, j * 2048 + t0: j * 2048 + t0 + n]

    def dma(eng, sem, dst, src, wait=(), **kw):
        return p.emit(eng, lambda e: e.dma_start(out=dst, in_=src, **kw),
                      wait=wait, inc=[(sem, 16)])

    dma("sync", "CLS", lp_sb[:, :], lpb[:, :])
    dma("sync", "CLS", bb_sb[:, :], bbb[:, :])
    dma("sync", "CLS", bthr_sb[:, :], bthr[:, :])
    dma("sync", "CLS", dfac_sb[:, :], dfac[:, :])
    dma("sync", "CLS", ones1_f[:, :], ones1[:, :])
    dma("sync", "CLS", id_f32[:, :], identf[:, :])
    dma("sync", "CLS", cv_sb[:, :], cvals[:, :])
    t_cls = p.val("CLS")
    dma("gpsimd", "CLG", id_bf[:, :], identf[:, :])
    dma("gpsimd", "CLG", ind_bf[:, :], indw[:, :])
    dma("gpsimd", "CLG", onesk_bf[:, :], onesk[:, :])
    dma("gpsimd", "CLG", w9_bf[:, :], w9b[:, :])
    dma("gpsimd", "CLG", wbufB[:, 4096:8192].rearrange("p (k m) -> p k m", k=KC),
        aab.ap().rearrange("k p m -> p k m"))
    dma("gpsimd", "CLG", wbufA[:, 4096:8192], btb[:, :])
    t_clg = p.val("CLG")

    p.emit("vector", lambda e: e.tensor_scalar_mul(lpneg[:, :], lp_sb[:, :], -1.0),
           wait=[("CLS", t_cls)], inc=[("EV", 1)])
    t = p.emit("scalar", lambda e: e.activation(c_row[:, :], dfac_sb[:, :], AF.Exp,
                                                scale=cv_sb[0:1, 2:3]),
               wait=[("CLS", t_cls)], inc=[("AC", 1)])
    t_ac_crow = t["AC"]
    t = p.emit("tensor", lambda e: e.matmul(P0[:, 0:6], ones1_f[:, :], c_row[:, :],
                                            start=True, stop=True),
               wait=[("AC", t_ac_crow), ("CLS", t_cls)], inc=[("MM", 1)])
    t_mm_cbc = t["MM"]
    t = p.emit("tensor", lambda e: e.matmul(P1[:, 0:1], ones1_f[:, :],
                                            bthr_sb[:, :], start=True, stop=True),
               inc=[("MM", 1)])
    t_mm_bthr = t["MM"]
    p.emit("scalar", lambda e: e.activation(c_bc[:, :], P0[:, 0:6], AF.Copy),
           wait=[("MM", t_mm_cbc)], inc=[("AC", 1)])
    t = p.emit("scalar", lambda e: e.activation(bthr_bc[:, :], P1[:, 0:1], AF.Copy),
               wait=[("MM", t_mm_bthr)], inc=[("AC", 1)])
    t_ac_bcasts = t["AC"]

    tok_xn_ready = 0
    if "ln" in phases:
        bfs = [bfst0, bfst1]
        xs = ["X0", "X1"]
        ld_tok = [0] * KC
        mm_tok = [0] * KC
        ac_tok = [0] * KC
        for kc in range(KC):
            sl = kc % 2
            w = []
            if kc >= 2:
                w.append(("MM", mm_tok[kc - 2]))
            tt = dma("gpsimd", xs[sl], bfs[sl][:, :],
                     x_fm[kc * 128:(kc + 1) * 128, :], wait=w)
            ld_tok[kc] = tt[xs[sl]]
            xb = bfs[sl]
            w = [(xs[sl], ld_tok[kc])]
            if kc >= 1:
                w.append(("MM", mm_tok[kc - 1]))
            tt = p.emit("scalar",
                        lambda e, xb=xb: e.activation(scr_bf[:, 4096:6144],
                                                      xb[:, :], AF.Square),
                        wait=w, inc=[("AC", 1)])
            ac_tok[kc] = tt["AC"]
            for nt in range(NT):
                c0 = nt * 512
                p.emit("tensor",
                       lambda e, xb=xb, c0=c0, kc=kc: e.matmul(
                           P0[0:1, c0:c0 + 512], onesk_bf[:, :],
                           xb[:, c0:c0 + 512],
                           start=(kc == 0), stop=(kc == KC - 1)),
                       wait=[(xs[sl], ld_tok[kc]), ("AC", t_ac_bcasts),
                             ("CLG", t_clg)])
            for nt in range(NT):
                c0 = nt * 512
                tt = p.emit("tensor",
                            lambda e, c0=c0, kc=kc: e.matmul(
                                P1[0:1, c0:c0 + 512], onesk_bf[:, :],
                                scr_bf[:, 4096 + c0:4096 + c0 + 512],
                                start=(kc == 0), stop=(kc == KC - 1)),
                            wait=[("AC", ac_tok[kc])],
                            inc=[("MM", 1)] if nt == NT - 1 else ())
            mm_tok[kc] = tt["MM"]
        t_mm_stats = mm_tok[KC - 1]

        RA = xf32[0:1, 0:2048]
        RB = xf32[0:1, 2048:4096]
        RC = stf[0:1, :]
        p.emit("scalar", lambda e: e.activation(RA, P0[0:1, :], AF.Copy,
                                                scale=cv_sb[0:1, 1:2]),
               wait=[("MM", t_mm_stats)], inc=[("AC", 1)])
        t = p.emit("scalar", lambda e: e.activation(RB, P1[0:1, :], AF.Copy,
                                                    scale=cv_sb[0:1, 1:2]),
                   inc=[("AC", 1)])
        t_ac_rows = t["AC"]
        p.emit("vector", lambda e: e.tensor_mul(RC, RA, RA),
               wait=[("AC", t_ac_rows)], inc=[("EV", 1)])
        t = p.emit("vector", lambda e: e.tensor_sub(RB, RB, RC), inc=[("EV", 1)])
        t_ev_var = t["EV"]
        t = p.emit("scalar", lambda e: e.activation(RC, RB, AF.Sqrt,
                                                    bias=cv_sb[0:1, 0:1]),
                   wait=[("EV", t_ev_var)], inc=[("AC", 1)])
        t_ac_sd = t["AC"]
        p.emit("vector", lambda e: e.reciprocal(RB, RC),
               wait=[("AC", t_ac_sd)], inc=[("EV", 1)])
        t = p.emit("vector", lambda e: e.tensor_mul(RC, RA, RB), inc=[("EV", 1)])
        t_ev_rows = t["EV"]
        for nt in range(NT):
            c0 = nt * 512
            tt = p.emit("tensor",
                        lambda e, c0=c0: e.matmul(
                            P1[:, c0:c0 + 512], ones1_f[:, :],
                            xf32[0:1, 2048 + c0:2048 + c0 + 512],
                            start=True, stop=True),
                        wait=[("EV", t_ev_rows)],
                        inc=[("MM", 1)] if nt == NT - 1 else ())
        t_mm_bc1 = tt["MM"]
        t = p.emit("scalar", lambda e: e.activation(scr_bf[:, 0:2048], P1[:, :],
                                                    AF.Copy),
                   wait=[("MM", t_mm_bc1)], inc=[("AC", 1)])
        t_ac_rbc = t["AC"]
        for nt in range(NT):
            c0 = nt * 512
            tt = p.emit("tensor",
                        lambda e, c0=c0: e.matmul(P0[:, c0:c0 + 512],
                                                  ones1_f[:, :],
                                                  stf[0:1, c0:c0 + 512],
                                                  start=True, stop=True),
                        wait=[("EV", t_ev_rows)],
                        inc=[("MM", 1)] if nt == NT - 1 else ())
        t_mm_bc2 = tt["MM"]
        t = p.emit("scalar", lambda e: e.activation(scr_bf[:, 2048:4096], P0[:, :],
                                                    AF.Copy),
                   wait=[("MM", t_mm_bc2)], inc=[("AC", 1)])
        t_ac_mrbc = t["AC"]

        ld1_tok = [0] * KC
        ev1_tok = [0] * KC
        for kc in range(KC):
            sl = kc % 2
            w = [("MM", t_mm_stats)]
            if kc >= 2:
                w.append(("EV", ev1_tok[kc - 2]))
            tt = dma("gpsimd", xs[sl], bfs[sl][:, :],
                     x_fm[kc * 128:(kc + 1) * 128, :], wait=w)
            ld1_tok[kc] = tt[xs[sl]]
            xb = bfs[sl]
            p.emit("vector",
                   lambda e, xb=xb, kc=kc: e.tensor_tensor(
                       xn(kc, 0, 2048), xb[:, :], scr_bf[:, 0:2048], ALU.mult),
                   wait=[(xs[sl], ld1_tok[kc]), ("MM", t_mm_bc2),
                         ("AC", max(t_ac_rbc, t_ac_mrbc))])
            tt = p.emit("vector",
                        lambda e, kc=kc: e.tensor_tensor(
                            xn(kc, 0, 2048), xn(kc, 0, 2048),
                            scr_bf[:, 2048:4096], ALU.subtract),
                        inc=[("EV", 1)])
            ev1_tok[kc] = tt["EV"]
        tok_xn_ready = ev1_tok[KC - 1]
        if debug:
            for kc in range(KC):
                dma("sync", "DBG", xn_d[kc, :, :], xn(kc, 0, 2048),
                    wait=[("EV", ev1_tok[kc])])
    t_dbg_ln = p.val("DBG")

    tok_wh = 0
    t_mm_moe_last = 0
    if "moe" in phases:
        for nt in range(NT):
            c0 = nt * 512
            for kc in range(KC):
                tt = p.emit("tensor",
                            lambda e, kc=kc, c0=c0: e.matmul(
                                P0[:, c0:c0 + 512],
                                wbufB[:, 4096 + kc * 128:4096 + (kc + 1) * 128],
                                xn(kc, c0, 512),
                                start=(kc == 0), stop=(kc == KC - 1)),
                            wait=[("EV", tok_xn_ready), ("CLG", t_clg)],
                            inc=[("MM", 1)] if (nt == NT - 1 and kc == KC - 1)
                            else ())
        t_mm_h = tt["MM"]
        for nt in range(NT):
            c0 = nt * 512
            for kc in range(KC):
                tt = p.emit("tensor",
                            lambda e, kc=kc, c0=c0: e.matmul(
                                P1[0:9, c0:c0 + 512],
                                w9_bf[:, kc * 9:(kc + 1) * 9],
                                xn(kc, c0, 512),
                                start=(kc == 0), stop=(kc == KC - 1)),
                            inc=[("MM", 1)] if (nt == NT - 1 and kc == KC - 1)
                            else ())
        t_mm_g9 = tt["MM"]
        t = p.emit("vector", lambda e: e.tensor_copy(stf[:, :], P0[:, :]),
                   wait=[("MM", t_mm_h)], inc=[("EV", 1)])
        t_ev_h = t["EV"]
        t = p.emit("scalar", lambda e: e.activation(scr_bf[0:9, 6144:8192],
                                                    P1[0:9, :], AF.Copy),
                   wait=[("MM", t_mm_g9)], inc=[("AC", 1)])
        t_ac_g9 = t["AC"]
        for ti in range(16):
            tt = p.emit("tensor",
                        lambda e, ti=ti: e.transpose(
                            P0b[:, ti * 12: ti * 12 + 9],
                            scr_bf[0:9, 6144 + ti * 128: 6144 + (ti + 1) * 128],
                            id_bf[0:9, 0:9]),
                        wait=[("AC", t_ac_g9), ("EV", t_ev_h)],
                        inc=[("MM", 1)] if ti == 15 else ())
        t_mm_tr = tt["MM"]
        gtf_in = bass.AP(P0b, 0, [[4096, 128], [12, 16], [1, 9]])
        gtf_out = bass.AP(scr, 1024, [[6144, 128], [9, 16], [1, 9]])
        t = p.emit("scalar", lambda e: e.activation(gtf_out, gtf_in, AF.Copy),
                   wait=[("MM", t_mm_tr)], inc=[("AC", 1)])
        t_ac_gtf = t["AC"]

        def gtf(a, b):
            return scr[:, 1024 + a:1024 + b]

        mxs = scr[:, 1200:1201]
        ex8 = scr[:, 1204:1212]
        sg = scr[:, 1216:1217]
        sg1 = scr[:, 1220:1221]
        ws = scr[:, 1224:1225]
        mk1 = scr[:, 1228:1229]
        rw = scr[:, 1232:1233]
        wt = scr[:, 1236:1244]
        tr_tok = 0
        t_ac_ex_prev = 0
        for ti in range(16):
            b = ti * 9
            w0 = [("AC", t_ac_gtf)]
            if ti > 0:
                w0.append(("MM", tr_tok))
                w0.append(("AC", t_ac_ex_prev))
            tt = p.emit("vector",
                        lambda e, b=b: e.reduce_max(mxs, gtf(b, b + 8),
                                                    axis=AX.X, negate=True),
                        wait=w0, inc=[("EV", 1)])
            t_ev_mx = tt["EV"]
            tt = p.emit("scalar",
                        lambda e, b=b: e.activation(ex8, gtf(b, b + 8), AF.Exp,
                                                    bias=mxs[:, 0:1]),
                        wait=[("EV", t_ev_mx)], inc=[("AC", 1)])
            t_ac_ex = tt["AC"]
            t_ac_ex_prev = t_ac_ex
            p.emit("vector", lambda e: e.reduce_sum(ws, ex8, axis=AX.X),
                   wait=[("AC", t_ac_ex)])
            p.emit("vector", lambda e: e.reciprocal(rw, ws))
            tt = p.emit("vector", lambda e: e.tensor_scalar(ex8, ex8, rw[:, 0:1],
                                                            None, ALU.mult),
                        inc=[("EV", 1)])
            t_ev_prob = tt["EV"]
            tt = p.emit("scalar",
                        lambda e, b=b: e.activation(sg, gtf(b + 8, b + 9),
                                                    AF.Sigmoid,
                                                    bias=bthr_bc[:, 0:1]),
                        wait=[("EV", t_ev_prob)], inc=[("AC", 1)])
            t_ac_sg = tt["AC"]
            p.emit("vector", lambda e: e.tensor_scalar_mul(sg1, sg, 0.125),
                   wait=[("AC", t_ac_sg)])
            p.emit("vector", lambda e: e.tensor_scalar(ex8, ex8, sg1[:, 0:1],
                                                       None, ALU.subtract))
            p.emit("vector", lambda e: e.tensor_scalar_max(ex8, ex8, 0.0))
            p.emit("vector", lambda e: e.reduce_sum(ws, ex8, axis=AX.X))
            p.emit("vector", lambda e: e.tensor_single_scalar(mk1, ws, 0.0,
                                                              ALU.is_equal))
            p.emit("vector", lambda e: e.tensor_add(ws, ws, mk1))
            p.emit("vector", lambda e: e.reciprocal(rw, ws))
            tt = p.emit("vector", lambda e: e.tensor_scalar(wt, ex8, rw[:, 0:1],
                                                            None, ALU.mult),
                        inc=[("EV", 1)])
            t_ev_wt = tt["EV"]
            tt = p.emit("tensor",
                        lambda e, ti=ti: e.transpose(
                            P1[0:8, ti * 128:(ti + 1) * 128], wt, id_f32[:, :]),
                        wait=[("EV", t_ev_wt)], inc=[("MM", 1)])
            tr_tok = tt["MM"]
        t = p.emit("scalar", lambda e: e.activation(scr_bf[0:8, 0:2048],
                                                    P1[0:8, :], AF.Copy),
                   wait=[("MM", tr_tok)], inc=[("AC", 1)])
        t_ac_wfm = t["AC"]
        for nt in range(NT):
            c0 = nt * 512
            tt = p.emit("tensor",
                        lambda e, c0=c0: e.matmul(
                            P0[:, c0:c0 + 512], ind_bf[:, :],
                            scr_bf[0:8, c0:c0 + 512], start=True, stop=True),
                        wait=[("AC", max(t_ac_wfm, t_ac_gtf))],
                        inc=[("MM", 1)] if nt == NT - 1 else ())
        t_mm_ind = tt["MM"]
        t = p.emit("vector", lambda e: e.tensor_tensor(scr_bf[:, 4096:6144],
                                                       stf[:, :], P0[:, :],
                                                       ALU.mult),
                   wait=[("MM", t_mm_ind)], inc=[("EV", 1)])
        tok_wh = t["EV"]
        t_mm_moe_last = t_mm_ind
        if debug:
            dma("sync", "DBG", wh_d[:, :], scr_bf[:, 4096:6144],
                wait=[("EV", tok_wh)])
            dma("sync", "DBG", h_d[:, :], stf[:, :], wait=[("EV", t_ev_h)])
            dma("sync", "DBG", gt_d[:, :], scr[:, 1024:1168],
                wait=[("AC", t_ac_gtf)])
            dma("sync", "DBG", wf_d[:, :], scr_bf[0:8, 0:2048],
                wait=[("AC", t_ac_wfm)])
    t_dbg_moe = p.val("DBG")

    t_mm_g_last = t_mm_moe_last
    tok_psum_free = []
    if "g" in phases:
        wh = scr_bf[:, 4096:6144]
        gw_tok = [0] * KC
        grp_tok = [0] * KC
        evac_tok = [0] * KC
        st_tok = [0] * KC

        def emit_gload(hc):
            w = [("CLG", t_clg)]
            if hc >= 2:
                w.append(("MM", grp_tok[hc - 2]))
            buf = [wbufA, wbufB][hc % 2]
            sem = ["WA", "WB"][hc % 2]
            tt = dma("gpsimd", sem,
                     buf[:, 0:4096].rearrange("p (k m) -> p k m", k=KC),
                     wbb[hc].rearrange("k p m -> p k m"), wait=w)
            gw_tok[hc] = tt[sem]

        emit_gload(0)
        emit_gload(1)
        for hc in range(KC):
            buf = [wbufA, wbufB][hc % 2]
            sem = ["WA", "WB"][hc % 2]
            P = PS[hc % 2]
            wbase = [(sem, gw_tok[hc]),
                     ("EV", max(tok_xn_ready, tok_wh)), ("DBG", t_dbg_moe)]
            if hc >= 2:
                wbase.append(("AC", evac_tok[hc - 2]))
            for nt in range(NT):
                c0 = nt * 512
                for kc in range(KC):
                    p.emit("tensor",
                           lambda e, buf=buf, P=P, kc=kc, c0=c0: e.matmul(
                               P[:, c0:c0 + 512],
                               buf[:, kc * 128:(kc + 1) * 128],
                               xn(kc, c0, 512), start=(kc == 0), stop=False),
                           wait=wbase if (nt == 0 and kc == 0) else ())
                tt = p.emit("tensor",
                            lambda e, P=P, hc=hc, c0=c0: e.matmul(
                                P[:, c0:c0 + 512],
                                wbufA[:, 4096 + hc * 128:4096 + (hc + 1) * 128],
                                wh[0:128, c0:c0 + 512], start=False, stop=True),
                            inc=[("MM", 1)] if nt == NT - 1 else ())
            grp_tok[hc] = tt["MM"]
            if hc + 2 < KC:
                emit_gload(hc + 2)
            w = [("MM", grp_tok[hc])]
            if hc >= 1:
                w.append(("GT", st_tok[hc - 1]))
            tt = p.emit("scalar",
                        lambda e, P=P, hc=hc: e.activation(
                            stf[:, :], P[:, :], AF.Silu,
                            bias=bb_sb[:, hc:hc + 1]),
                        wait=w, inc=[("AC", 1)])
            evac_tok[hc] = tt["AC"]
            tt = dma("sync", "GT", g_d[hc, :, :], stf[:, :],
                     wait=[("AC", evac_tok[hc])])
            st_tok[hc] = tt["GT"]
        t_mm_g_last = grp_tok[KC - 1]
        tok_psum_free = [("AC", evac_tok[KC - 1])]
    t_gt_final = p.val("GT")

    t_ev_b_last = 0
    t_ac_b_last = 0
    if "b" in phases:
        slots = [(bfst0, 0), (bfst0, 1), (bfst1, 0), (bfst1, 1)]
        slot_ss = [0, 0, 0, 0]
        slot_idx = [0]
        wld_tok = [0] * SCN
        grp_au = [[0, 0] for _ in range(SCN)]
        grp_gp = [[0, 0] for _ in range(SCN)]
        ev_iter = {}
        ac_iter = {}
        delta = scr[:, 0:1024]
        au_f = scr[:, 1024:2048]
        dpA = scr[:, 2048:3072]
        dpB = scr[:, 3072:4096]
        ts_a = scr[:, 4096:5120]
        ts_b = scr[:, 5120:6144]

        def emit_bload(sc):
            w = []
            if sc >= 2:
                w.append(("MM", grp_gp[sc - 2][1]))
            else:
                w.append(("MM", t_mm_g_last))
            buf = [wbufA, wbufB][sc % 2]
            sem = ["WA", "WB"][sc % 2]
            dma("gpsimd", sem,
                buf[:, 0:4096].rearrange("p (k m) -> p k m", k=KC),
                upb[sc].rearrange("k p m -> p k m"), wait=w)
            tt = dma("gpsimd", sem,
                     buf[:, 4096:8192].rearrange("p (k m) -> p k m", k=KC),
                     gab[sc].rearrange("k p m -> p k m"))
            wld_tok[sc] = tt[sem]

        emit_bload(0)
        emit_bload(1)
        prev_ev = 0
        for sc in range(SCN):
            buf = [wbufA, wbufB][sc % 2]
            sem = ["WA", "WB"][sc % 2]
            for th in range(2):
                it = sc * 2 + th
                P = PS[it % 2]
                wb = [(sem, wld_tok[sc]), ("EV", tok_xn_ready),
                      ("DBG", t_dbg_moe)]
                if it < 2 and tok_psum_free:
                    wb += tok_psum_free
                if it >= 2:
                    wb.append(("EV", ev_iter.get(it - 2, 0)))
                    wb.append(("AC", ac_iter.get(it - 2, 0)))
                t0 = th * 1024
                for ntl in range(2):
                    c0 = t0 + ntl * 512
                    pc = ntl * 512
                    for kc in range(KC):
                        tt = p.emit("tensor",
                                    lambda e, P=P, buf=buf, kc=kc, c0=c0, pc=pc:
                                    e.matmul(P[:, pc:pc + 512],
                                             buf[:, kc * 128:(kc + 1) * 128],
                                             xn(kc, c0, 512),
                                             start=(kc == 0),
                                             stop=(kc == KC - 1)),
                                    wait=wb if (ntl == 0 and kc == 0) else (),
                                    inc=[("MM", 1)] if (ntl == 1 and
                                                        kc == KC - 1) else ())
                grp_au[sc][th] = tt["MM"]
                for ntl in range(2):
                    c0 = t0 + ntl * 512
                    pc = 1024 + ntl * 512
                    for kc in range(KC):
                        tt = p.emit("tensor",
                                    lambda e, P=P, buf=buf, kc=kc, c0=c0, pc=pc:
                                    e.matmul(P[:, pc:pc + 512],
                                             buf[:, 4096 + kc * 128:
                                                 4096 + (kc + 1) * 128],
                                             xn(kc, c0, 512),
                                             start=(kc == 0),
                                             stop=(kc == KC - 1)),
                                    inc=[("MM", 1)] if (ntl == 1 and
                                                        kc == KC - 1) else ())
                grp_gp[sc][th] = tt["MM"]

                w = [("MM", grp_au[sc][th])]
                if it >= 1:
                    w.append(("AC", ac_iter.get(it - 1, 0)))
                tt = p.emit("vector",
                            lambda e, P=P: e.tensor_copy(au_f, P[:, 0:1024]),
                            wait=w, inc=[("EV", 1)])
                t_ev_auf = tt["EV"]
                t = p.emit("scalar",
                           lambda e, P=P, sc=sc: e.activation(
                               delta, P[:, 1024:2048], AF.Identity,
                               bias=lpneg[:, sc:sc + 1]),
                           wait=[("MM", grp_gp[sc][th]), ("EV", prev_ev)],
                           inc=[("AC", 1)])
                t_ac_delta = t["AC"]

                def store_op(op_j, src_reg, ev_tok, sc=sc, th=th):
                    sidx = slot_idx[0] % 4
                    sl_buf, sl_h = slots[sidx]
                    ssem = "TS%d" % sidx
                    slot_idx[0] += 1
                    dst = sl_buf[:, sl_h * 1024:(sl_h + 1) * 1024]
                    w = [(ssem, slot_ss[sidx]), ("EV", ev_tok)]
                    if op_j == 0:
                        p.emit("scalar",
                               lambda e, dst=dst: e.activation(dst, au_f,
                                                               AF.Copy),
                               wait=w, inc=[("AC", 1)])
                    else:
                        p.emit("scalar",
                               lambda e, dst=dst, src_reg=src_reg, op_j=op_j:
                               e.activation(dst, src_reg, AF.Copy,
                                            scale=c_bc[:, op_j - 1:op_j]),
                               wait=w, inc=[("AC", 1)])
                    t_ac_c = p.val("AC")
                    tt = dma("sync", ssem,
                             tmp_d[op_j, sc, :, th * 1024:(th + 1) * 1024],
                             dst, wait=[("AC", t_ac_c)])
                    slot_ss[sidx] = tt[ssem]
                    return t_ac_c

                store_op(0, None, t_ev_auf)
                tt = p.emit("vector", lambda e: e.tensor_mul(ts_a, delta, au_f),
                            wait=[("AC", t_ac_delta), ("EV", t_ev_auf)],
                            inc=[("EV", 1)])
                t_ac1 = store_op(1, ts_a, tt["EV"])
                p.emit("vector", lambda e: e.tensor_mul(dpA, delta, delta))
                tt = p.emit("vector", lambda e: e.tensor_mul(ts_b, dpA, au_f),
                            inc=[("EV", 1)])
                t_ac2 = store_op(2, ts_b, tt["EV"])
                p.emit("vector", lambda e: e.tensor_mul(dpB, dpA, delta))
                tt = p.emit("vector", lambda e: e.tensor_mul(ts_a, dpB, au_f),
                            wait=[("AC", t_ac1)], inc=[("EV", 1)])
                t_ac3 = store_op(3, ts_a, tt["EV"])
                p.emit("vector", lambda e: e.tensor_mul(dpA, dpB, delta))
                tt = p.emit("vector", lambda e: e.tensor_mul(ts_b, dpA, au_f),
                            wait=[("AC", t_ac2)], inc=[("EV", 1)])
                t_ac4 = store_op(4, ts_b, tt["EV"])
                p.emit("vector", lambda e: e.tensor_mul(dpB, dpA, delta))
                t = p.emit("scalar", lambda e: e.activation(ts_b, delta, AF.Abs),
                           wait=[("AC", t_ac4)])
                t_ac_abs = t["AC"]
                p.emit("vector",
                       lambda e: e.tensor_single_scalar(ts_b, ts_b, 2.5,
                                                        ALU.is_le),
                       wait=[("AC", t_ac_abs)])
                p.emit("vector", lambda e: e.tensor_mul(dpB, dpB, ts_b))
                tt = p.emit("vector", lambda e: e.tensor_mul(ts_a, dpB, au_f),
                            wait=[("AC", t_ac3)], inc=[("EV", 1)])
                t_ac5 = store_op(5, ts_a, tt["EV"])
                p.emit("vector", lambda e: e.tensor_mul(dpA, dpB, delta))
                tt = p.emit("vector", lambda e: e.tensor_mul(ts_b, dpA, au_f),
                            wait=[("AC", t_ac4)], inc=[("EV", 1)])
                t_ac6 = store_op(6, ts_b, tt["EV"])
                prev_ev = tt["EV"]
                ev_iter[it] = prev_ev
                ac_iter[it] = p.val("AC")
            if sc + 2 < SCN:
                emit_bload(sc + 2)
        t_ev_b_last = prev_ev
        t_ac_b_last = p.val("AC")
    t_ts_final = [p.val("TS%d" % i) for i in range(4)]

    t_mm_d_last = p.val("MM")
    t_ac_d_last = p.val("AC")
    t_ev_d_last = p.val("EV")
    if "d" in phases:
        wld = [0] * FCN
        grp_gate = {}
        grp_up = {}
        ac_sg = {}
        ev_a = {}
        ss_a = [0, 0]

        def emit_dload(fc):
            w = []
            if fc >= 2:
                w.append(("MM", grp_up.get((fc - 2, 1), 0)))
            else:
                w.append(("MM", p.val("MM")))
            buf = [wbufA, wbufB][fc % 2]
            sem = ["WA", "WB"][fc % 2]
            dma("gpsimd", sem,
                buf[:, 0:4096].rearrange("p (k m) -> p k m", k=KC),
                wgb[fc].rearrange("k p m -> p k m"), wait=w)
            tt = dma("gpsimd", sem,
                     buf[:, 4096:8192].rearrange("p (k m) -> p k m", k=KC),
                     wub[fc].rearrange("k p m -> p k m"))
            wld[fc] = tt[sem]

        emit_dload(0)
        emit_dload(1)
        for fc in range(FCN):
            buf = [wbufA, wbufB][fc % 2]
            sem = ["WA", "WB"][fc % 2]
            abuf = [bfst0, bfst1][fc % 2]
            asem = ["AS0", "AS1"][fc % 2]
            for th in range(2):
                it = fc * 2 + th
                P = PS[it % 2]
                wb = [(sem, wld[fc]), ("EV", tok_xn_ready)]
                if it >= 2:
                    wb.append(("AC", ac_sg.get(it - 2, 0)))
                    wb.append(("EV", ev_a.get(it - 2, 0)))
                else:
                    wb.append(("EV", t_ev_b_last))
                    wb.append(("AC", t_ac_b_last))
                t0 = th * 1024
                for ntl in range(2):
                    c0 = t0 + ntl * 512
                    for kc in range(KC):
                        tt = p.emit("tensor",
                                    lambda e, P=P, buf=buf, kc=kc, c0=c0,
                                    pc=ntl * 512: e.matmul(
                                        P[:, pc:pc + 512],
                                        buf[:, kc * 128:(kc + 1) * 128],
                                        xn(kc, c0, 512),
                                        start=(kc == 0), stop=(kc == KC - 1)),
                                    wait=wb if (ntl == 0 and kc == 0) else (),
                                    inc=[("MM", 1)] if (ntl == 1 and
                                                        kc == KC - 1) else ())
                grp_gate[(fc, th)] = tt["MM"]
                for ntl in range(2):
                    c0 = t0 + ntl * 512
                    for kc in range(KC):
                        tt = p.emit("tensor",
                                    lambda e, P=P, buf=buf, kc=kc, c0=c0,
                                    pc=1024 + ntl * 512: e.matmul(
                                        P[:, pc:pc + 512],
                                        buf[:, 4096 + kc * 128:
                                            4096 + (kc + 1) * 128],
                                        xn(kc, c0, 512),
                                        start=(kc == 0), stop=(kc == KC - 1)),
                                    inc=[("MM", 1)] if (ntl == 1 and
                                                        kc == KC - 1) else ())
                grp_up[(fc, th)] = tt["MM"]

                sgs = scr[:, th * 1024:(th + 1) * 1024]
                w = [("MM", grp_gate[(fc, th)])]
                if it >= 2:
                    w.append(("EV", ev_a.get(it - 2, 0)))
                else:
                    w.append(("EV", t_ev_b_last))
                t = p.emit("scalar",
                           lambda e, P=P, sgs=sgs: e.activation(
                               sgs, P[:, 0:1024], AF.Silu),
                           wait=w, inc=[("AC", 1)])
                ac_sg[it] = t["AC"]
                w = [("MM", grp_up[(fc, th)]), ("AC", ac_sg[it])]
                if fc >= 2:
                    w.append((asem, ss_a[fc % 2]))
                else:
                    w += [("TS%d" % i, t_ts_final[i]) for i in range(4)]
                t = p.emit("vector",
                           lambda e, P=P, sgs=sgs, abuf=abuf, t0=t0:
                           e.tensor_tensor(abuf[:, t0:t0 + 1024], sgs,
                                           P[:, 1024:2048], ALU.mult),
                           wait=w, inc=[("EV", 1)])
                ev_a[it] = t["EV"]
            tt = dma("sync", asem, a_d[fc, :, :], abuf[:, :],
                     wait=[("EV", ev_a[fc * 2 + 1])])
            ss_a[fc % 2] = tt[asem]
            if fc + 2 < FCN:
                emit_dload(fc + 2)
        t_mm_d_last = grp_up[(FCN - 1, 1)]
        t_ac_d_last = p.val("AC")
        t_ev_d_last = p.val("EV")
    t_as_final = [p.val("AS0"), p.val("AS1")]

    if "c" in phases:
        ac_prev_pass = 0
        grp_prev_pass = 0
        for pz in range(2):
            w_ops = [("MM", t_mm_d_last if pz == 0 else p.val("MM")),
                     ("DBG", t_dbg_ln)]
            w_ops += [("TS%d" % i, t_ts_final[i]) for i in range(4)]
            for op in range(7):
                for scl in range(4):
                    j = op * 4 + scl
                    sc_g = pz * 4 + scl
                    tt = dma("sync", "OL", cop(j, 0, 2048),
                             tmp_d[op, sc_g, :, :], wait=w_ops)
            t_ol_ops = tt["OL"]

            wtok = [0] * KC
            grp = [0] * KC
            ac_ev = [0] * KC
            gs_tok = [0] * KC
            gs_prev = p.val("GS")

            def emit_cload(hc, pz=pz, grp=grp, wtok=wtok, gpp=grp_prev_pass):
                w = []
                if hc >= 2:
                    w.append(("MM", grp[hc - 2]))
                else:
                    w.append(("MM", t_mm_d_last if pz == 0 else gpp))
                buf = [wbufA, wbufB][hc % 2]
                sem = ["WA", "WB"][hc % 2]
                tt = dma("gpsimd", sem,
                         buf[:, 0:3584].rearrange("p (j m) -> p j m", j=28),
                         w7b[pz, hc].rearrange("o s p m -> p (o s) m"), wait=w)
                wtok[hc] = tt[sem]

            emit_cload(0)
            emit_cload(1)
            for hc in range(KC):
                buf = [wbufA, wbufB][hc % 2]
                sem = ["WA", "WB"][hc % 2]
                P = PS[hc % 2]
                wb = [(sem, wtok[hc]), ("OL", t_ol_ops)]
                if hc >= 2:
                    wb.append(("AC", ac_ev[hc - 2]))
                elif pz == 1:
                    wb.append(("AC", ac_prev_pass))
                else:
                    wb.append(("AC", t_ac_d_last))
                    wb.append(("EV", t_ev_d_last))
                for nt in range(NT):
                    c0 = nt * 512
                    for op in range(7):
                        for scl in range(4):
                            j = op * 4 + scl
                            tt = p.emit(
                                "tensor",
                                lambda e, P=P, buf=buf, j=j, c0=c0,
                                st=(op == 0 and scl == 0),
                                sp=(op == 6 and scl == 3): e.matmul(
                                    P[:, c0:c0 + 512],
                                    buf[:, j * 128:(j + 1) * 128],
                                    cop(j, c0, 512), start=st, stop=sp),
                                wait=wb if (nt == 0 and op == 0 and scl == 0)
                                else (),
                                inc=[("MM", 1)] if (nt == NT - 1 and op == 6
                                                    and scl == 3) else ())
                grp[hc] = tt["MM"]
                if hc + 2 < KC:
                    emit_cload(hc + 2)
                w = [("MM", grp[hc]),
                     ("GS", gs_tok[hc - 1] if hc >= 1 else gs_prev)]
                if pz == 0 and hc == 0:
                    w.append(("GT", t_gt_final))
                t = p.emit("scalar",
                           lambda e, P=P: e.activation(stf[:, :], P[:, :],
                                                       AF.Copy),
                           wait=w, inc=[("AC", 1)])
                ac_ev[hc] = t["AC"]
                kw = {} if pz == 0 else {"accum_op": ALU.add}
                tt = dma("gpsimd", "GS", s_d[hc, :, :], stf[:, :],
                         wait=[("AC", ac_ev[hc])], **kw)
                gs_tok[hc] = tt["GS"]
            ac_prev_pass = ac_ev[KC - 1]
            grp_prev_pass = grp[KC - 1]

    if "e" in phases:
        SLD = 26624
        GLD = 28672
        s_ld = xf32[:, SLD:SLD + 2048]
        g_ld = xf32[:, GLD:GLD + 2048]
        ac_prev = p.val("AC")
        grp_prev_pass = p.val("MM")
        gs_e2_final = 0
        for tz in range(3):
            w_ops = [("MM", grp_prev_pass), ("AS0", t_as_final[0]),
                     ("AS1", t_as_final[1])]
            for fsc in range(26):
                fcg = tz * 26 + fsc
                tt = dma("sync", "OL", cop(fsc, 0, 2048), a_d[fcg, :, :],
                         wait=w_ops)
            t_ol_aops = tt["OL"]

            wtok = [0] * KC
            grp = [0] * KC
            ac_ev = [0] * KC
            gs_tok = [0] * KC
            ev_chain = [0] * KC
            os_out = [0] * KC
            gs_prev = p.val("GS")

            def emit_eload(hc, tz=tz, grp=grp, wtok=wtok, gpp=grp_prev_pass):
                w = []
                if hc >= 2:
                    w.append(("MM", grp[hc - 2]))
                else:
                    w.append(("MM", gpp))
                buf = [wbufA, wbufB][hc % 2]
                sem = ["WA", "WB"][hc % 2]
                tt = dma("gpsimd", sem,
                         buf[:, 0:3328].rearrange("p (f m) -> p f m", f=26),
                         wdb[tz, hc].rearrange("f p m -> p f m"), wait=w)
                wtok[hc] = tt[sem]

            emit_eload(0)
            emit_eload(1)
            for hc in range(KC):
                buf = [wbufA, wbufB][hc % 2]
                sem = ["WA", "WB"][hc % 2]
                P = PS[hc % 2]
                wb = [(sem, wtok[hc]), ("OL", t_ol_aops)]
                if tz < 2:
                    if hc >= 2:
                        wb.append(("AC", ac_ev[hc - 2]))
                    else:
                        wb.append(("AC", ac_prev))
                else:
                    if hc >= 2:
                        wb.append(("EV", ev_chain[hc - 2]))
                    else:
                        wb.append(("AC", ac_prev))
                for nt in range(NT):
                    c0 = nt * 512
                    for fsc in range(26):
                        tt = p.emit("tensor",
                                    lambda e, P=P, buf=buf, fsc=fsc, c0=c0:
                                    e.matmul(P[:, c0:c0 + 512],
                                             buf[:, fsc * 128:(fsc + 1) * 128],
                                             cop(fsc, c0, 512),
                                             start=(fsc == 0),
                                             stop=(fsc == 25)),
                                    wait=wb if (nt == 0 and fsc == 0) else (),
                                    inc=[("MM", 1)] if (nt == NT - 1 and
                                                        fsc == 25) else ())
                grp[hc] = tt["MM"]
                if hc + 2 < KC:
                    emit_eload(hc + 2)

                if tz < 2:
                    w = [("MM", grp[hc]),
                         ("GS", gs_tok[hc - 1] if hc >= 1 else gs_prev)]
                    t = p.emit("scalar",
                               lambda e, P=P: e.activation(stf[:, :], P[:, :],
                                                           AF.Copy),
                               wait=w, inc=[("AC", 1)])
                    ac_ev[hc] = t["AC"]
                    tt = dma("gpsimd", "GS", s_d[hc, :, :], stf[:, :],
                             wait=[("AC", ac_ev[hc])], accum_op=ALU.add)
                    gs_tok[hc] = tt["GS"]
                else:
                    wl = [("GS", gs_e2_final)]
                    if hc >= 1:
                        wl.append(("EV", ev_chain[hc - 1]))
                    dma("sync", "EL", s_ld, s_d[hc, :, :], wait=wl)
                    dma("sync", "EL", g_ld, g_d[hc, :, :],
                        wait=[("GT", t_gt_final)])
                    wx = [("OS", os_out[hc - 1])] if hc >= 1 else \
                        [("GS", gs_e2_final)]
                    tt = dma("sync", "EL", stf[:, :],
                             x_fm[hc * 128:(hc + 1) * 128, :], wait=wx)
                    t_el = tt["EL"]
                    p.emit("vector",
                           lambda e, P=P: e.tensor_tensor(s_ld, P[:, :], s_ld,
                                                          ALU.add),
                           wait=[("MM", grp[hc]), ("EL", t_el)])
                    p.emit("vector",
                           lambda e: e.tensor_tensor(g_ld, s_ld, g_ld,
                                                     ALU.mult))
                    tt = p.emit("vector",
                                lambda e: e.tensor_tensor(stf[:, :], g_ld,
                                                          stf[:, :], ALU.add),
                                inc=[("EV", 1)])
                    ev_chain[hc] = tt["EV"]
                    tt = dma("sync", "OS", out_d[hc, :, :], stf[:, :],
                             wait=[("EV", ev_chain[hc])])
                    os_out[hc] = tt["OS"]
            if tz == 1:
                gs_e2_final = p.val("GS")
            if tz < 2:
                ac_prev = ac_ev[KC - 1]
            grp_prev_pass = grp[KC - 1]

    fin_sync = [(s, p.val(s)) for s in
                ("OS", "GT", "TS0", "TS1", "TS2", "TS3", "AS0", "AS1",
                 "OL", "EL", "CLS", "DBG")]
    p.emit("sync", lambda e: e.nop(), wait=fin_sync)
    fin_gp = [(s, p.val(s)) for s in
              ("GS", "CLG", "WA", "WB", "X0", "X1")]
    p.emit("gpsimd", lambda e: e.nop(), wait=fin_gp)

    from contextlib import ExitStack
    with ExitStack() as ctx:
        for name in SEMS:
            sem_handles[name] = ctx.enter_context(nc.semaphore(name))
        with nc.Block() as block:
            for eng in Prog.ENGINES:
                if p.items[eng]:
                    getattr(block, eng)(_emitter(p.items[eng], sem_handles))
    return nc


def host_prep(inputs):
    f4 = np.float32
    up = np.asarray(inputs["L0_attn_up_w"], f4)
    ga = np.asarray(inputs["L0_attn_gate_w"], f4)
    lp = np.asarray(inputs["L0_attn_local_point"], f4)
    lao = np.asarray(inputs["L0_attn_local_approx_output"], f4)
    fw = np.asarray(inputs["L0_attn_fuse_weight"], f4)
    df = np.asarray(inputs["L0_attn_discount_factor"], f4)
    wg = np.asarray(inputs["L0_attn_ffn_gate_w"], f4)
    wu = np.asarray(inputs["L0_attn_ffn_up_w"], f4)
    wd = np.asarray(inputs["L0_attn_ffn_down_w"], f4)
    wbase = np.asarray(inputs["L0_ffn_W_base"], f4)
    bbase = np.asarray(inputs["L0_ffn_b_base"], f4)
    wgate = np.asarray(inputs["L0_ffn_W_gate"], f4)
    wthr = np.asarray(inputs["L0_ffn_W_thr"], f4)
    bthr = np.asarray(inputs["L0_ffn_b_thr"], f4)
    Am = np.asarray(inputs["L0_ffn_A"], f4)
    Bm = np.asarray(inputs["L0_ffn_B"], f4)

    shared = {}
    shared["upb"] = np.ascontiguousarray(
        up.T.reshape(KC, 128, SCN, 128).transpose(2, 0, 1, 3))
    shared["gab"] = np.ascontiguousarray(
        ga.T.reshape(KC, 128, SCN, 128).transpose(2, 0, 1, 3))
    W7 = np.concatenate([lao.T[None], fw.transpose(2, 1, 0)], 0)
    shared["w7b"] = np.ascontiguousarray(
        W7.reshape(7, 2, 4, 128, KC, 128).transpose(1, 4, 0, 2, 3, 5))
    shared["wgb"] = np.ascontiguousarray(
        wg.T.reshape(KC, 128, FCN, 128).transpose(2, 0, 1, 3))
    shared["wub"] = np.ascontiguousarray(
        wu.T.reshape(KC, 128, FCN, 128).transpose(2, 0, 1, 3))
    shared["wdb"] = np.ascontiguousarray(
        wd.T.reshape(3, 26, 128, KC, 128).transpose(0, 3, 1, 2, 4))
    shared["wbb"] = np.ascontiguousarray(
        wbase.T.reshape(KC, 128, KC, 128).transpose(2, 0, 1, 3))
    shared["aab"] = np.ascontiguousarray(
        Am.transpose(2, 0, 1).reshape(H, 128).reshape(KC, 128, 128))
    shared["btb"] = np.ascontiguousarray(Bm.transpose(0, 2, 1).reshape(128, H))
    w9 = np.concatenate([wgate, wthr], 0)
    shared["w9b"] = np.ascontiguousarray(
        w9.T.reshape(KC, 128, 9).transpose(1, 0, 2).reshape(128, 288))
    shared["lpb"] = np.ascontiguousarray(lp.reshape(8, 128).T)
    shared["bbb"] = np.ascontiguousarray(bbase.reshape(KC, 128).T)
    shared["bthr"] = np.ascontiguousarray(bthr.reshape(1, 1))
    shared["dfac"] = np.ascontiguousarray(df.reshape(1, 6))
    shared["onesk"] = np.ones((128, 1), f4)
    shared["ones1"] = np.ones((1, 128), f4)
    shared["identf"] = np.eye(128, dtype=f4)
    ind = np.zeros((8, 128), f4)
    for e in range(8):
        ind[e, e * 16:(e + 1) * 16] = 2.0
    shared["indw"] = ind
    shared["cvals"] = np.array([[EPS, 1.0 / H, -1.0, 0.0]], f4)

    x = np.asarray(inputs["L0_attn_hidden_states"], f4).reshape(16384, H)
    xfms = [np.ascontiguousarray(x[c * NTOK:(c + 1) * NTOK].T)
            for c in range(NCORE)]
    return shared, xfms


def kernel(**inputs):
    shared, xfms = host_prep(inputs)
    nc = build()
    in_maps = [dict(shared, x_fm=xfms[c]) for c in range(NCORE)]
    res = run_bass_kernel_spmd(nc, in_maps, core_ids=list(range(NCORE)))
    out = np.empty((16384, H), np.float32)
    for c in range(NCORE):
        oc = np.asarray(res.results[c]["out"], np.float32).reshape(H, NTOK)
        out[c * NTOK:(c + 1) * NTOK] = oc.T
    return out.reshape(4, 4096, 4096)


# version 02
# speedup vs baseline: 1.1018x; 1.1018x over previous
import numpy as np
import ml_dtypes

import concourse.bass as bass
import concourse.mybir as mybir
from concourse.bass_utils import run_bass_kernel_spmd

F32 = mybir.dt.float32
BF16 = mybir.dt.bfloat16
AF = mybir.ActivationFunctionType
ALU = mybir.AluOpType
AX = mybir.AxisListType

NCORE = 8
NTOK = 2048
H = 4096
KC = 32
S = 1024
SCN = 8
FF = 9984
FCN = 78
NT = 4
EPS = 1e-5

ALL_PHASES = ("ln", "moe", "g", "b", "d", "c", "e")

SEMS = ("CLS", "CLG", "X0", "X1", "WA", "WB", "EL",
        "OL0", "OL1", "OL2", "OL3", "OL4", "OL5", "OL6",
        "TS0", "TS1", "TS2", "TS3", "AS0", "AS1", "GT", "OS", "DBG", "GS",
        "MM", "EV", "AC")


class Prog:
    ENGINES = ("sync", "gpsimd", "tensor", "vector", "scalar")

    def __init__(self):
        self.items = {e: [] for e in self.ENGINES}
        self.cnt = {s: 0 for s in SEMS}

    AUTO = {"vector": "EV", "scalar": "AC"}

    def emit(self, eng, fn, wait=(), inc=()):
        inc = list(inc)
        wait = list(wait)
        auto = self.AUTO.get(eng)
        if auto is not None:
            wait.append((auto, self.cnt[auto]))
            if not any(s == auto for (s, n) in inc):
                inc.append((auto, 1))
        w = [(s, v) for (s, v) in wait if v > 0]
        for s, n in inc:
            self.cnt[s] += n
        self.items[eng].append((fn, w, list(inc)))
        return {s: self.cnt[s] for (s, n) in inc}

    def val(self, s):
        return self.cnt[s]


def _emitter(items, sems):
    def run(eng):
        last = {}
        for fn, waits, incs in items:
            for s, v in waits:
                if last.get(s, -1) < v:
                    eng.wait_ge(sems[s], v)
                    last[s] = v
            inst = fn(eng)
            for s, n in incs:
                inst = inst.then_inc(sems[s], n)
    return run


def build(phases=ALL_PHASES, debug=False):
    nc = bass.Bass("TRN2", target_bir_lowering=False)

    EI = dict(kind="ExternalInput")
    x_fm = nc.dram_tensor("x_fm", [H, NTOK], F32, **EI)
    upb = nc.dram_tensor("upb", [SCN, KC, 128, 128], F32, **EI)
    gab = nc.dram_tensor("gab", [SCN, KC, 128, 128], F32, **EI)
    w7b = nc.dram_tensor("w7b", [2, KC, 7, 4, 128, 128], F32, **EI)
    wgb = nc.dram_tensor("wgb", [FCN, KC, 128, 128], F32, **EI)
    wub = nc.dram_tensor("wub", [FCN, KC, 128, 128], F32, **EI)
    wdb = nc.dram_tensor("wdb", [3, KC, 26, 128, 128], F32, **EI)
    wbb = nc.dram_tensor("wbb", [KC, KC, 128, 128], F32, **EI)
    aab = nc.dram_tensor("aab", [KC, 128, 128], F32, **EI)
    btb = nc.dram_tensor("btb", [128, H], F32, **EI)
    w9b = nc.dram_tensor("w9b", [128, 288], F32, **EI)
    lpb = nc.dram_tensor("lpb", [128, 8], F32, **EI)
    bbb = nc.dram_tensor("bbb", [128, 32], F32, **EI)
    bthr = nc.dram_tensor("bthr", [1, 1], F32, **EI)
    dfac = nc.dram_tensor("dfac", [1, 6], F32, **EI)
    onesk = nc.dram_tensor("onesk", [128, 1], F32, **EI)
    ones1 = nc.dram_tensor("ones1", [1, 128], F32, **EI)
    identf = nc.dram_tensor("identf", [128, 128], F32, **EI)
    indw = nc.dram_tensor("indw", [8, 128], F32, **EI)
    cvals = nc.dram_tensor("cvals", [1, 4], F32, **EI)

    out_d = nc.dram_tensor("out", [KC, 128, NTOK], F32, kind="ExternalOutput")

    skind = "ExternalOutput" if debug else "Internal"
    tmp_d = nc.dram_tensor("tmp_d", [7, SCN, 128, NTOK], BF16, kind=skind)
    a_d = nc.dram_tensor("a_d", [FCN, 128, NTOK], BF16, kind=skind)
    g_d = nc.dram_tensor("g_d", [KC, 128, NTOK], F32, kind=skind)
    s_d = nc.dram_tensor("s_d", [KC, 128, NTOK], F32, kind=skind)
    if debug:
        xn_d = nc.dram_tensor("xn_d", [KC, 128, NTOK], BF16, kind="ExternalOutput")
        wh_d = nc.dram_tensor("wh_d", [128, NTOK], BF16, kind="ExternalOutput")
        h_d = nc.dram_tensor("h_d", [128, NTOK], F32, kind="ExternalOutput")
        gt_d = nc.dram_tensor("gt_d", [128, 144], F32, kind="ExternalOutput")
        wf_d = nc.dram_tensor("wf_d", [8, NTOK], BF16, kind="ExternalOutput")

    xnbuf = nc.alloc_sbuf_tensor("xnbuf", [128, 65536], BF16)
    wbufA = nc.alloc_sbuf_tensor("wbufA", [128, 8192], BF16)
    wbufB = nc.alloc_sbuf_tensor("wbufB", [128, 8192], BF16)
    stf = nc.alloc_sbuf_tensor("stf", [128, 2048], F32)
    bfst0 = nc.alloc_sbuf_tensor("bfst0", [128, 2048], BF16)
    bfst1 = nc.alloc_sbuf_tensor("bfst1", [128, 2048], BF16)
    scr = nc.alloc_sbuf_tensor("scr", [128, 6144], F32)
    id_bf = nc.alloc_sbuf_tensor("id_bf", [128, 128], BF16)
    id_f32 = nc.alloc_sbuf_tensor("id_f32", [128, 128], F32)
    ind_bf = nc.alloc_sbuf_tensor("ind_bf", [8, 128], BF16)
    onesk_bf = nc.alloc_sbuf_tensor("onesk_bf", [128, 1], BF16)
    ones1_f = nc.alloc_sbuf_tensor("ones1_f", [1, 128], F32)
    w9_bf = nc.alloc_sbuf_tensor("w9_bf", [128, 288], BF16)
    lp_sb = nc.alloc_sbuf_tensor("lp_sb", [128, 8], F32)
    lpneg = nc.alloc_sbuf_tensor("lpneg", [128, 8], F32)
    bb_sb = nc.alloc_sbuf_tensor("bb_sb", [128, 32], F32)
    bthr_sb = nc.alloc_sbuf_tensor("bthr_sb", [1, 1], F32)
    bthr_bc = nc.alloc_sbuf_tensor("bthr_bc", [128, 1], F32)
    dfac_sb = nc.alloc_sbuf_tensor("dfac_sb", [1, 6], F32)
    c_row = nc.alloc_sbuf_tensor("c_row", [1, 6], F32)
    c_bc = nc.alloc_sbuf_tensor("c_bc", [128, 6], F32)
    cv_sb = nc.alloc_sbuf_tensor("cv_sb", [1, 4], F32)

    xf32 = xnbuf.bitcast(F32)
    scr_bf = scr.bitcast(BF16)

    P0 = nc.alloc_psum_tensor("P0", [128, 2048], F32)
    P1 = nc.alloc_psum_tensor("P1", [128, 2048], F32)
    P0b = P0.bitcast(BF16)
    PS = [P0, P1]

    p = Prog()
    sem_handles = {}

    def xn(kc, t0, n):
        return xnbuf[:, kc * 2048 + t0: kc * 2048 + t0 + n]

    def cop(j, t0, n):
        return xnbuf[:, j * 2048 + t0: j * 2048 + t0 + n]

    def dma(eng, sem, dst, src, wait=(), **kw):
        return p.emit(eng, lambda e: e.dma_start(out=dst, in_=src, **kw),
                      wait=wait, inc=[(sem, 16)])

    dma("sync", "CLS", lp_sb[:, :], lpb[:, :])
    dma("sync", "CLS", bb_sb[:, :], bbb[:, :])
    dma("sync", "CLS", bthr_sb[:, :], bthr[:, :])
    dma("sync", "CLS", dfac_sb[:, :], dfac[:, :])
    dma("sync", "CLS", ones1_f[:, :], ones1[:, :])
    dma("sync", "CLS", id_f32[:, :], identf[:, :])
    dma("sync", "CLS", cv_sb[:, :], cvals[:, :])
    t_cls = p.val("CLS")
    dma("gpsimd", "CLG", id_bf[:, :], identf[:, :])
    dma("gpsimd", "CLG", ind_bf[:, :], indw[:, :])
    dma("gpsimd", "CLG", onesk_bf[:, :], onesk[:, :])
    dma("gpsimd", "CLG", w9_bf[:, :], w9b[:, :])
    dma("gpsimd", "CLG", wbufB[:, 4096:8192].rearrange("p (k m) -> p k m", k=KC),
        aab.ap().rearrange("k p m -> p k m"))
    dma("gpsimd", "CLG", wbufA[:, 4096:8192], btb[:, :])
    t_clg = p.val("CLG")

    p.emit("vector", lambda e: e.tensor_scalar_mul(lpneg[:, :], lp_sb[:, :], -1.0),
           wait=[("CLS", t_cls)], inc=[("EV", 1)])
    t = p.emit("scalar", lambda e: e.activation(c_row[:, :], dfac_sb[:, :], AF.Exp,
                                                scale=cv_sb[0:1, 2:3]),
               wait=[("CLS", t_cls)], inc=[("AC", 1)])
    t_ac_crow = t["AC"]
    t = p.emit("tensor", lambda e: e.matmul(P0[:, 0:6], ones1_f[:, :], c_row[:, :],
                                            start=True, stop=True),
               wait=[("AC", t_ac_crow), ("CLS", t_cls)], inc=[("MM", 1)])
    t_mm_cbc = t["MM"]
    t = p.emit("tensor", lambda e: e.matmul(P1[:, 0:1], ones1_f[:, :],
                                            bthr_sb[:, :], start=True, stop=True),
               inc=[("MM", 1)])
    t_mm_bthr = t["MM"]
    p.emit("scalar", lambda e: e.activation(c_bc[:, :], P0[:, 0:6], AF.Copy),
           wait=[("MM", t_mm_cbc)], inc=[("AC", 1)])
    t = p.emit("scalar", lambda e: e.activation(bthr_bc[:, :], P1[:, 0:1], AF.Copy),
               wait=[("MM", t_mm_bthr)], inc=[("AC", 1)])
    t_ac_bcasts = t["AC"]

    tok_xn_ready = 0
    ev1_tok = [0] * KC
    if "ln" in phases:
        xs = ["X0", "X1"]
        ld_tok = [0] * KC
        mm_tok = [0] * KC
        ac_tok = [0] * KC
        for kc in range(KC):
            sl = kc % 2
            w = []
            if kc >= 2:
                w.append(("MM", mm_tok[kc - 2]))
            tt = dma("gpsimd", xs[sl], xn(kc, 0, 2048),
                     x_fm[kc * 128:(kc + 1) * 128, :], wait=w)
            ld_tok[kc] = tt[xs[sl]]
            w = [(xs[sl], ld_tok[kc])]
            if kc >= 1:
                w.append(("MM", mm_tok[kc - 1]))
            tt = p.emit("scalar",
                        lambda e, kc=kc: e.activation(scr_bf[:, 4096:6144],
                                                      xn(kc, 0, 2048),
                                                      AF.Square),
                        wait=w, inc=[("AC", 1)])
            ac_tok[kc] = tt["AC"]
            for nt in range(NT):
                c0 = nt * 512
                p.emit("tensor",
                       lambda e, c0=c0, kc=kc: e.matmul(
                           P0[0:1, c0:c0 + 512], onesk_bf[:, :],
                           xn(kc, c0, 512),
                           start=(kc == 0), stop=(kc == KC - 1)),
                       wait=[(xs[sl], ld_tok[kc]), ("AC", t_ac_bcasts),
                             ("CLG", t_clg)])
            for nt in range(NT):
                c0 = nt * 512
                tt = p.emit("tensor",
                            lambda e, c0=c0, kc=kc: e.matmul(
                                P1[0:1, c0:c0 + 512], onesk_bf[:, :],
                                scr_bf[:, 4096 + c0:4096 + c0 + 512],
                                start=(kc == 0), stop=(kc == KC - 1)),
                            wait=[("AC", ac_tok[kc])],
                            inc=[("MM", 1)] if nt == NT - 1 else ())
            mm_tok[kc] = tt["MM"]
        t_mm_stats = mm_tok[KC - 1]

        RA = scr[0:1, 2048:4096]
        RB = scr[0:1, 4096:6144]
        RC = stf[0:1, :]
        p.emit("scalar", lambda e: e.activation(RA, P0[0:1, :], AF.Copy,
                                                scale=cv_sb[0:1, 1:2]),
               wait=[("MM", t_mm_stats)], inc=[("AC", 1)])
        t = p.emit("scalar", lambda e: e.activation(RB, P1[0:1, :], AF.Copy,
                                                    scale=cv_sb[0:1, 1:2]),
                   inc=[("AC", 1)])
        t_ac_rows = t["AC"]
        p.emit("vector", lambda e: e.tensor_mul(RC, RA, RA),
               wait=[("AC", t_ac_rows)], inc=[("EV", 1)])
        t = p.emit("vector", lambda e: e.tensor_sub(RB, RB, RC), inc=[("EV", 1)])
        t_ev_var = t["EV"]
        t = p.emit("scalar", lambda e: e.activation(RC, RB, AF.Sqrt,
                                                    bias=cv_sb[0:1, 0:1]),
                   wait=[("EV", t_ev_var)], inc=[("AC", 1)])
        t_ac_sd = t["AC"]
        p.emit("vector", lambda e: e.reciprocal(RB, RC),
               wait=[("AC", t_ac_sd)], inc=[("EV", 1)])
        t = p.emit("vector", lambda e: e.tensor_mul(RC, RA, RB), inc=[("EV", 1)])
        t_ev_rows = t["EV"]
        for nt in range(NT):
            c0 = nt * 512
            tt = p.emit("tensor",
                        lambda e, c0=c0: e.matmul(
                            P1[:, c0:c0 + 512], ones1_f[:, :],
                            scr[0:1, 4096 + c0:4096 + c0 + 512],
                            start=True, stop=True),
                        wait=[("EV", t_ev_rows)],
                        inc=[("MM", 1)] if nt == NT - 1 else ())
        t_mm_bc1 = tt["MM"]
        t = p.emit("scalar", lambda e: e.activation(scr_bf[:, 0:2048], P1[:, :],
                                                    AF.Copy),
                   wait=[("MM", t_mm_bc1)], inc=[("AC", 1)])
        t_ac_rbc = t["AC"]
        for nt in range(NT):
            c0 = nt * 512
            tt = p.emit("tensor",
                        lambda e, c0=c0: e.matmul(P0[:, c0:c0 + 512],
                                                  ones1_f[:, :],
                                                  stf[0:1, c0:c0 + 512],
                                                  start=True, stop=True),
                        wait=[("EV", t_ev_rows)],
                        inc=[("MM", 1)] if nt == NT - 1 else ())
        t_mm_bc2 = tt["MM"]
        t = p.emit("scalar", lambda e: e.activation(scr_bf[:, 2048:4096], P0[:, :],
                                                    AF.Copy),
                   wait=[("MM", t_mm_bc2)], inc=[("AC", 1)])
        t_ac_mrbc = t["AC"]

        for kc in range(KC):
            p.emit("vector",
                   lambda e, kc=kc: e.tensor_tensor(
                       xn(kc, 0, 2048), xn(kc, 0, 2048), scr_bf[:, 0:2048],
                       ALU.mult),
                   wait=[("MM", t_mm_bc2),
                         ("AC", max(t_ac_rbc, t_ac_mrbc))])
            tt = p.emit("vector",
                        lambda e, kc=kc: e.tensor_tensor(
                            xn(kc, 0, 2048), xn(kc, 0, 2048),
                            scr_bf[:, 2048:4096], ALU.subtract),
                        inc=[("EV", 1)])
            ev1_tok[kc] = tt["EV"]
        tok_xn_ready = ev1_tok[KC - 1]
        if debug:
            for kc in range(KC):
                dma("sync", "DBG", xn_d[kc, :, :], xn(kc, 0, 2048),
                    wait=[("EV", ev1_tok[kc])])
    t_dbg_ln = p.val("DBG")

    tok_wh = 0
    t_mm_moe_last = 0
    if "moe" in phases:
        for nt in range(NT):
            c0 = nt * 512
            for kc in range(KC):
                tt = p.emit("tensor",
                            lambda e, kc=kc, c0=c0: e.matmul(
                                P0[:, c0:c0 + 512],
                                wbufB[:, 4096 + kc * 128:4096 + (kc + 1) * 128],
                                xn(kc, c0, 512),
                                start=(kc == 0), stop=(kc == KC - 1)),
                            wait=[("EV", ev1_tok[kc]), ("CLG", t_clg)],
                            inc=[("MM", 1)] if (nt == NT - 1 and kc == KC - 1)
                            else ())
        t_mm_h = tt["MM"]
        for nt in range(NT):
            c0 = nt * 512
            for kc in range(KC):
                tt = p.emit("tensor",
                            lambda e, kc=kc, c0=c0: e.matmul(
                                P1[0:9, c0:c0 + 512],
                                w9_bf[:, kc * 9:(kc + 1) * 9],
                                xn(kc, c0, 512),
                                start=(kc == 0), stop=(kc == KC - 1)),
                            wait=[("EV", ev1_tok[kc])],
                            inc=[("MM", 1)] if (nt == NT - 1 and kc == KC - 1)
                            else ())
        t_mm_g9 = tt["MM"]
        t = p.emit("vector", lambda e: e.tensor_copy(stf[:, :], P0[:, :]),
                   wait=[("MM", t_mm_h)], inc=[("EV", 1)])
        t_ev_h = t["EV"]
        t = p.emit("scalar", lambda e: e.activation(scr_bf[0:9, 6144:8192],
                                                    P1[0:9, :], AF.Copy),
                   wait=[("MM", t_mm_g9)], inc=[("AC", 1)])
        t_ac_g9 = t["AC"]
        for ti in range(16):
            tt = p.emit("tensor",
                        lambda e, ti=ti: e.transpose(
                            P0b[:, ti * 12: ti * 12 + 9],
                            scr_bf[0:9, 6144 + ti * 128: 6144 + (ti + 1) * 128],
                            id_bf[0:9, 0:9]),
                        wait=[("AC", t_ac_g9), ("EV", t_ev_h)],
                        inc=[("MM", 1)] if ti == 15 else ())
        t_mm_tr = tt["MM"]
        gtf_in = bass.AP(P0b, 0, [[4096, 128], [12, 16], [1, 9]])
        gtf_out = bass.AP(scr, 1024, [[6144, 128], [9, 16], [1, 9]])
        t = p.emit("scalar", lambda e: e.activation(gtf_out, gtf_in, AF.Copy),
                   wait=[("MM", t_mm_tr)], inc=[("AC", 1)])
        t_ac_gtf = t["AC"]

        gtf3 = bass.AP(scr, 1024, [[6144, 128], [9, 16], [1, 8]])
        thr2 = bass.AP(scr, 1024 + 8, [[6144, 128], [9, 16]])
        mx = scr[:, 1200:1216]
        mxb = bass.AP(scr, 1200, [[6144, 128], [1, 16], [0, 8]])
        ex = scr[:, 1232:1360]
        ex3 = bass.AP(scr, 1232, [[6144, 128], [8, 16], [1, 8]])
        wsv = scr[:, 1360:1376]
        rs = scr[:, 1376:1392]
        rsb = bass.AP(scr, 1376, [[6144, 128], [1, 16], [0, 8]])
        sgv = scr[:, 1392:1408]
        sgvb = bass.AP(scr, 1392, [[6144, 128], [1, 16], [0, 8]])
        mk = scr[:, 1408:1424]
        wtA = scr[:, 1440:1568]
        wtA3 = bass.AP(scr, 1440, [[6144, 128], [8, 16], [1, 8]])

        p.emit("vector", lambda e: e.reduce_max(mx, gtf3, axis=AX.X, negate=True),
               wait=[("AC", t_ac_gtf)])
        t = p.emit("vector", lambda e: e.tensor_tensor(ex3, gtf3, mxb, ALU.add),
                   inc=[("EV", 1)])
        t_ev1 = t["EV"]
        t = p.emit("scalar", lambda e: e.activation(ex, ex, AF.Exp),
                   wait=[("EV", t_ev1)], inc=[("AC", 1)])
        t_ac1 = t["AC"]
        p.emit("vector", lambda e: e.reduce_sum(wsv, ex3, axis=AX.X),
               wait=[("AC", t_ac1)])
        p.emit("vector", lambda e: e.reciprocal(rs, wsv))
        t = p.emit("vector", lambda e: e.tensor_tensor(ex3, ex3, rsb, ALU.mult),
                   inc=[("EV", 1)])
        t_ev2 = t["EV"]
        t = p.emit("scalar", lambda e: e.activation(sgv, thr2, AF.Sigmoid,
                                                    bias=bthr_bc[:, 0:1]),
                   inc=[("AC", 1)])
        t_ac_sg = t["AC"]
        p.emit("vector", lambda e: e.tensor_scalar_mul(sgv, sgv, 0.125),
               wait=[("AC", t_ac_sg)])
        p.emit("vector", lambda e: e.tensor_tensor(ex3, ex3, sgvb, ALU.subtract))
        p.emit("vector", lambda e: e.tensor_scalar_max(ex, ex, 0.0))
        p.emit("vector", lambda e: e.reduce_sum(wsv, ex3, axis=AX.X))
        p.emit("vector", lambda e: e.tensor_single_scalar(mk, wsv, 0.0,
                                                          ALU.is_equal))
        p.emit("vector", lambda e: e.tensor_add(wsv, wsv, mk))
        p.emit("vector", lambda e: e.reciprocal(rs, wsv))
        t = p.emit("vector", lambda e: e.tensor_tensor(wtA3, ex3, rsb, ALU.mult),
                   inc=[("EV", 1)])
        t_ev_wt = t["EV"]
        for ti in range(16):
            tt = p.emit("tensor",
                        lambda e, ti=ti: e.transpose(
                            P1[0:8, ti * 128:(ti + 1) * 128],
                            wtA[:, ti * 8:(ti + 1) * 8], id_f32[:, :]),
                        wait=[("EV", t_ev_wt)], inc=[("MM", 1)])
        tr_tok = tt["MM"]
        t = p.emit("scalar", lambda e: e.activation(scr_bf[0:8, 0:2048],
                                                    P1[0:8, :], AF.Copy),
                   wait=[("MM", tr_tok)], inc=[("AC", 1)])
        t_ac_wfm = t["AC"]
        for nt in range(NT):
            c0 = nt * 512
            tt = p.emit("tensor",
                        lambda e, c0=c0: e.matmul(
                            P0[:, c0:c0 + 512], ind_bf[:, :],
                            scr_bf[0:8, c0:c0 + 512], start=True, stop=True),
                        wait=[("AC", max(t_ac_wfm, t_ac_gtf))],
                        inc=[("MM", 1)] if nt == NT - 1 else ())
        t_mm_ind = tt["MM"]
        t = p.emit("vector", lambda e: e.tensor_tensor(scr_bf[:, 4096:6144],
                                                       stf[:, :], P0[:, :],
                                                       ALU.mult),
                   wait=[("MM", t_mm_ind)], inc=[("EV", 1)])
        tok_wh = t["EV"]
        t_mm_moe_last = t_mm_ind
        if debug:
            dma("sync", "DBG", wh_d[:, :], scr_bf[:, 4096:6144],
                wait=[("EV", tok_wh)])
            dma("sync", "DBG", h_d[:, :], stf[:, :], wait=[("EV", t_ev_h)])
            dma("sync", "DBG", gt_d[:, :], scr[:, 1024:1168],
                wait=[("AC", t_ac_gtf)])
            dma("sync", "DBG", wf_d[:, :], scr_bf[0:8, 0:2048],
                wait=[("AC", t_ac_wfm)])
    t_dbg_moe = p.val("DBG")

    t_mm_g_last = t_mm_moe_last
    tok_psum_free = []
    if "g" in phases:
        wh = scr_bf[:, 4096:6144]
        gw_tok = [0] * KC
        grp_tok = [0] * KC
        evac_tok = [0] * KC
        st_tok = [0] * KC

        def emit_gload(hc):
            w = [("CLG", t_clg)]
            if hc >= 2:
                w.append(("MM", grp_tok[hc - 2]))
            buf = [wbufA, wbufB][hc % 2]
            sem = ["WA", "WB"][hc % 2]
            tt = dma("gpsimd", sem,
                     buf[:, 0:4096].rearrange("p (k m) -> p k m", k=KC),
                     wbb[hc].rearrange("k p m -> p k m"), wait=w)
            gw_tok[hc] = tt[sem]

        emit_gload(0)
        emit_gload(1)
        for hc in range(KC):
            buf = [wbufA, wbufB][hc % 2]
            sem = ["WA", "WB"][hc % 2]
            P = PS[hc % 2]
            wbase = [(sem, gw_tok[hc]), ("EV", tok_wh), ("DBG", t_dbg_moe)]
            if hc >= 2:
                wbase.append(("AC", evac_tok[hc - 2]))
            for nt in range(NT):
                c0 = nt * 512
                for kc in range(KC):
                    w = wbase if (nt == 0 and kc == 0) else ()
                    if hc < 2 and nt == 0:
                        w = list(w) + [("EV", ev1_tok[kc])]
                    p.emit("tensor",
                           lambda e, buf=buf, P=P, kc=kc, c0=c0: e.matmul(
                               P[:, c0:c0 + 512],
                               buf[:, kc * 128:(kc + 1) * 128],
                               xn(kc, c0, 512), start=(kc == 0), stop=False),
                           wait=w)
                tt = p.emit("tensor",
                            lambda e, P=P, hc=hc, c0=c0: e.matmul(
                                P[:, c0:c0 + 512],
                                wbufA[:, 4096 + hc * 128:4096 + (hc + 1) * 128],
                                wh[0:128, c0:c0 + 512], start=False, stop=True),
                            inc=[("MM", 1)] if nt == NT - 1 else ())
            grp_tok[hc] = tt["MM"]
            if hc + 2 < KC:
                emit_gload(hc + 2)
            w = [("MM", grp_tok[hc])]
            if hc >= 1:
                w.append(("GT", st_tok[hc - 1]))
            tt = p.emit("scalar",
                        lambda e, P=P, hc=hc: e.activation(
                            stf[:, :], P[:, :], AF.Silu,
                            bias=bb_sb[:, hc:hc + 1]),
                        wait=w, inc=[("AC", 1)])
            evac_tok[hc] = tt["AC"]
            tt = dma("sync", "GT", g_d[hc, :, :], stf[:, :],
                     wait=[("AC", evac_tok[hc])])
            st_tok[hc] = tt["GT"]
        t_mm_g_last = grp_tok[KC - 1]
        tok_psum_free = [("AC", evac_tok[KC - 1])]
    t_gt_final = p.val("GT")

    t_ev_b_last = 0
    t_ac_b_last = 0
    if "b" in phases:
        slots = [(bfst0, 0), (bfst0, 1), (bfst1, 0), (bfst1, 1)]
        slot_ss = [0, 0, 0, 0]
        slot_idx = [0]
        wld_tok = [0] * SCN
        grp_au = [[0, 0] for _ in range(SCN)]
        grp_gp = [[0, 0] for _ in range(SCN)]
        ev_iter = {}
        ac_iter = {}
        delta = scr[:, 0:1024]
        au_f = scr[:, 1024:2048]
        dpA = scr[:, 2048:3072]
        dpB = scr[:, 3072:4096]
        ts_a = scr[:, 4096:5120]
        ts_b = scr[:, 5120:6144]

        def emit_bload(sc):
            w = []
            if sc >= 2:
                w.append(("MM", grp_gp[sc - 2][1]))
            else:
                w.append(("MM", t_mm_g_last))
            buf = [wbufA, wbufB][sc % 2]
            sem = ["WA", "WB"][sc % 2]
            dma("gpsimd", sem,
                buf[:, 0:4096].rearrange("p (k m) -> p k m", k=KC),
                upb[sc].rearrange("k p m -> p k m"), wait=w)
            tt = dma("gpsimd", sem,
                     buf[:, 4096:8192].rearrange("p (k m) -> p k m", k=KC),
                     gab[sc].rearrange("k p m -> p k m"))
            wld_tok[sc] = tt[sem]

        emit_bload(0)
        emit_bload(1)
        prev_ev = 0
        for sc in range(SCN):
            buf = [wbufA, wbufB][sc % 2]
            sem = ["WA", "WB"][sc % 2]
            for th in range(2):
                it = sc * 2 + th
                P = PS[it % 2]
                wb = [(sem, wld_tok[sc]), ("EV", tok_xn_ready),
                      ("DBG", t_dbg_moe)]
                if it < 2 and tok_psum_free:
                    wb += tok_psum_free
                if it >= 2:
                    wb.append(("EV", ev_iter.get(it - 2, 0)))
                    wb.append(("AC", ac_iter.get(it - 2, 0)))
                t0 = th * 1024
                for ntl in range(2):
                    c0 = t0 + ntl * 512
                    pc = ntl * 512
                    for kc in range(KC):
                        tt = p.emit("tensor",
                                    lambda e, P=P, buf=buf, kc=kc, c0=c0, pc=pc:
                                    e.matmul(P[:, pc:pc + 512],
                                             buf[:, kc * 128:(kc + 1) * 128],
                                             xn(kc, c0, 512),
                                             start=(kc == 0),
                                             stop=(kc == KC - 1)),
                                    wait=wb if (ntl == 0 and kc == 0) else (),
                                    inc=[("MM", 1)] if (ntl == 1 and
                                                        kc == KC - 1) else ())
                grp_au[sc][th] = tt["MM"]
                for ntl in range(2):
                    c0 = t0 + ntl * 512
                    pc = 1024 + ntl * 512
                    for kc in range(KC):
                        tt = p.emit("tensor",
                                    lambda e, P=P, buf=buf, kc=kc, c0=c0, pc=pc:
                                    e.matmul(P[:, pc:pc + 512],
                                             buf[:, 4096 + kc * 128:
                                                 4096 + (kc + 1) * 128],
                                             xn(kc, c0, 512),
                                             start=(kc == 0),
                                             stop=(kc == KC - 1)),
                                    inc=[("MM", 1)] if (ntl == 1 and
                                                        kc == KC - 1) else ())
                grp_gp[sc][th] = tt["MM"]

                w = [("MM", grp_au[sc][th])]
                if it >= 1:
                    w.append(("AC", ac_iter.get(it - 1, 0)))
                tt = p.emit("vector",
                            lambda e, P=P: e.tensor_copy(au_f, P[:, 0:1024]),
                            wait=w, inc=[("EV", 1)])
                t_ev_auf = tt["EV"]
                t = p.emit("scalar",
                           lambda e, P=P, sc=sc: e.activation(
                               delta, P[:, 1024:2048], AF.Identity,
                               bias=lpneg[:, sc:sc + 1]),
                           wait=[("MM", grp_gp[sc][th]), ("EV", prev_ev)],
                           inc=[("AC", 1)])
                t_ac_delta = t["AC"]

                def store_op(op_j, src_reg, ev_tok, sc=sc, th=th):
                    sidx = slot_idx[0] % 4
                    sl_buf, sl_h = slots[sidx]
                    ssem = "TS%d" % sidx
                    slot_idx[0] += 1
                    dst = sl_buf[:, sl_h * 1024:(sl_h + 1) * 1024]
                    w = [(ssem, slot_ss[sidx]), ("EV", ev_tok)]
                    if op_j == 0:
                        p.emit("scalar",
                               lambda e, dst=dst: e.activation(dst, au_f,
                                                               AF.Copy),
                               wait=w, inc=[("AC", 1)])
                    else:
                        p.emit("scalar",
                               lambda e, dst=dst, src_reg=src_reg, op_j=op_j:
                               e.activation(dst, src_reg, AF.Copy,
                                            scale=c_bc[:, op_j - 1:op_j]),
                               wait=w, inc=[("AC", 1)])
                    t_ac_c = p.val("AC")
                    tt = dma("sync", ssem,
                             tmp_d[op_j, sc, :, th * 1024:(th + 1) * 1024],
                             dst, wait=[("AC", t_ac_c)])
                    slot_ss[sidx] = tt[ssem]
                    return t_ac_c

                store_op(0, None, t_ev_auf)
                tt = p.emit("vector", lambda e: e.tensor_mul(ts_a, delta, au_f),
                            wait=[("AC", t_ac_delta), ("EV", t_ev_auf)],
                            inc=[("EV", 1)])
                t_ac1 = store_op(1, ts_a, tt["EV"])
                p.emit("vector", lambda e: e.tensor_mul(dpA, delta, delta))
                tt = p.emit("vector", lambda e: e.tensor_mul(ts_b, dpA, au_f),
                            inc=[("EV", 1)])
                t_ac2 = store_op(2, ts_b, tt["EV"])
                p.emit("vector", lambda e: e.tensor_mul(dpB, dpA, delta))
                tt = p.emit("vector", lambda e: e.tensor_mul(ts_a, dpB, au_f),
                            wait=[("AC", t_ac1)], inc=[("EV", 1)])
                t_ac3 = store_op(3, ts_a, tt["EV"])
                p.emit("vector", lambda e: e.tensor_mul(dpA, dpB, delta))
                tt = p.emit("vector", lambda e: e.tensor_mul(ts_b, dpA, au_f),
                            wait=[("AC", t_ac2)], inc=[("EV", 1)])
                t_ac4 = store_op(4, ts_b, tt["EV"])
                p.emit("vector", lambda e: e.tensor_mul(dpB, dpA, delta))
                t = p.emit("scalar", lambda e: e.activation(ts_b, delta, AF.Abs),
                           wait=[("AC", t_ac4)])
                t_ac_abs = t["AC"]
                p.emit("vector",
                       lambda e: e.tensor_single_scalar(ts_b, ts_b, 2.5,
                                                        ALU.is_le),
                       wait=[("AC", t_ac_abs)])
                p.emit("vector", lambda e: e.tensor_mul(dpB, dpB, ts_b))
                tt = p.emit("vector", lambda e: e.tensor_mul(ts_a, dpB, au_f),
                            wait=[("AC", t_ac3)], inc=[("EV", 1)])
                t_ac5 = store_op(5, ts_a, tt["EV"])
                p.emit("vector", lambda e: e.tensor_mul(dpA, dpB, delta))
                tt = p.emit("vector", lambda e: e.tensor_mul(ts_b, dpA, au_f),
                            wait=[("AC", t_ac4)], inc=[("EV", 1)])
                t_ac6 = store_op(6, ts_b, tt["EV"])
                prev_ev = tt["EV"]
                ev_iter[it] = prev_ev
                ac_iter[it] = p.val("AC")
            if sc + 2 < SCN:
                emit_bload(sc + 2)
        t_ev_b_last = prev_ev
        t_ac_b_last = p.val("AC")
    t_ts_final = [p.val("TS%d" % i) for i in range(4)]

    t_mm_d_last = p.val("MM")
    t_ac_d_last = p.val("AC")
    t_ev_d_last = p.val("EV")
    if "d" in phases:
        wld = [0] * FCN
        grp_gate = {}
        grp_up = {}
        ac_sg = {}
        ev_a = {}
        ss_a = [0, 0]

        def emit_dload(fc):
            w = []
            if fc >= 2:
                w.append(("MM", grp_up.get((fc - 2, 1), 0)))
            else:
                w.append(("MM", p.val("MM")))
            buf = [wbufA, wbufB][fc % 2]
            sem = ["WA", "WB"][fc % 2]
            dma("gpsimd", sem,
                buf[:, 0:4096].rearrange("p (k m) -> p k m", k=KC),
                wgb[fc].rearrange("k p m -> p k m"), wait=w)
            tt = dma("gpsimd", sem,
                     buf[:, 4096:8192].rearrange("p (k m) -> p k m", k=KC),
                     wub[fc].rearrange("k p m -> p k m"))
            wld[fc] = tt[sem]

        emit_dload(0)
        emit_dload(1)
        for fc in range(FCN):
            buf = [wbufA, wbufB][fc % 2]
            sem = ["WA", "WB"][fc % 2]
            abuf = [bfst0, bfst1][fc % 2]
            asem = ["AS0", "AS1"][fc % 2]
            for th in range(2):
                it = fc * 2 + th
                P = PS[it % 2]
                wb = [(sem, wld[fc]), ("EV", tok_xn_ready)]
                if it >= 2:
                    wb.append(("AC", ac_sg.get(it - 2, 0)))
                    wb.append(("EV", ev_a.get(it - 2, 0)))
                else:
                    wb.append(("EV", t_ev_b_last))
                    wb.append(("AC", t_ac_b_last))
                t0 = th * 1024
                for ntl in range(2):
                    c0 = t0 + ntl * 512
                    for kc in range(KC):
                        tt = p.emit("tensor",
                                    lambda e, P=P, buf=buf, kc=kc, c0=c0,
                                    pc=ntl * 512: e.matmul(
                                        P[:, pc:pc + 512],
                                        buf[:, kc * 128:(kc + 1) * 128],
                                        xn(kc, c0, 512),
                                        start=(kc == 0), stop=(kc == KC - 1)),
                                    wait=wb if (ntl == 0 and kc == 0) else (),
                                    inc=[("MM", 1)] if (ntl == 1 and
                                                        kc == KC - 1) else ())
                grp_gate[(fc, th)] = tt["MM"]
                for ntl in range(2):
                    c0 = t0 + ntl * 512
                    for kc in range(KC):
                        tt = p.emit("tensor",
                                    lambda e, P=P, buf=buf, kc=kc, c0=c0,
                                    pc=1024 + ntl * 512: e.matmul(
                                        P[:, pc:pc + 512],
                                        buf[:, 4096 + kc * 128:
                                            4096 + (kc + 1) * 128],
                                        xn(kc, c0, 512),
                                        start=(kc == 0), stop=(kc == KC - 1)),
                                    inc=[("MM", 1)] if (ntl == 1 and
                                                        kc == KC - 1) else ())
                grp_up[(fc, th)] = tt["MM"]

                sgs = scr[:, th * 1024:(th + 1) * 1024]
                w = [("MM", grp_gate[(fc, th)])]
                if it >= 2:
                    w.append(("EV", ev_a.get(it - 2, 0)))
                else:
                    w.append(("EV", t_ev_b_last))
                t = p.emit("scalar",
                           lambda e, P=P, sgs=sgs: e.activation(
                               sgs, P[:, 0:1024], AF.Silu),
                           wait=w, inc=[("AC", 1)])
                ac_sg[it] = t["AC"]
                w = [("MM", grp_up[(fc, th)]), ("AC", ac_sg[it])]
                if fc >= 2:
                    w.append((asem, ss_a[fc % 2]))
                else:
                    w += [("TS%d" % i, t_ts_final[i]) for i in range(4)]
                t = p.emit("vector",
                           lambda e, P=P, sgs=sgs, abuf=abuf, t0=t0:
                           e.tensor_tensor(abuf[:, t0:t0 + 1024], sgs,
                                           P[:, 1024:2048], ALU.mult),
                           wait=w, inc=[("EV", 1)])
                ev_a[it] = t["EV"]
            tt = dma("sync", asem, a_d[fc, :, :], abuf[:, :],
                     wait=[("EV", ev_a[fc * 2 + 1])])
            ss_a[fc % 2] = tt[asem]
            if fc + 2 < FCN:
                emit_dload(fc + 2)
        t_mm_d_last = grp_up[(FCN - 1, 1)]
        t_ac_d_last = p.val("AC")
        t_ev_d_last = p.val("EV")
    t_as_final = [p.val("AS0"), p.val("AS1")]

    if "c" in phases:
        ac_prev_pass = 0
        grp_prev_pass = 0
        for pz in range(2):
            w_ops = [("MM", t_mm_d_last if pz == 0 else p.val("MM")),
                     ("DBG", t_dbg_ln)]
            w_ops += [("TS%d" % i, t_ts_final[i]) for i in range(4)]
            ol_tok = [0] * 7
            for op in range(7):
                sem_o = "OL%d" % op
                tt = dma("sync", sem_o,
                         xnbuf[:, op * 8192:(op + 1) * 8192].rearrange(
                             "p (s t) -> p s t", s=4),
                         tmp_d[op, pz * 4:(pz + 1) * 4].rearrange(
                             "s p t -> p s t"),
                         wait=w_ops)
                ol_tok[op] = tt[sem_o]

            wtok = [0] * KC
            grp = [0] * KC
            ac_ev = [0] * KC
            gs_tok = [0] * KC
            gs_prev = p.val("GS")

            def emit_cload(hc, pz=pz, grp=grp, wtok=wtok, gpp=grp_prev_pass):
                w = []
                if hc >= 2:
                    w.append(("MM", grp[hc - 2]))
                else:
                    w.append(("MM", t_mm_d_last if pz == 0 else gpp))
                buf = [wbufA, wbufB][hc % 2]
                sem = ["WA", "WB"][hc % 2]
                tt = dma("gpsimd", sem,
                         buf[:, 0:3584].rearrange("p (j m) -> p j m", j=28),
                         w7b[pz, hc].rearrange("o s p m -> p (o s) m"), wait=w)
                wtok[hc] = tt[sem]

            emit_cload(0)
            emit_cload(1)
            for hc in range(KC):
                buf = [wbufA, wbufB][hc % 2]
                sem = ["WA", "WB"][hc % 2]
                P = PS[hc % 2]
                wb = [(sem, wtok[hc])]
                if hc >= 2:
                    wb.append(("AC", ac_ev[hc - 2]))
                elif pz == 1:
                    wb.append(("AC", ac_prev_pass))
                else:
                    wb.append(("AC", t_ac_d_last))
                    wb.append(("EV", t_ev_d_last))
                for nt in range(NT):
                    c0 = nt * 512
                    for op in range(7):
                        for scl in range(4):
                            j = op * 4 + scl
                            w = wb if (nt == 0 and op == 0 and scl == 0) else ()
                            if nt == 0 and scl == 0:
                                w = list(w) + [("OL%d" % op, ol_tok[op])]
                            tt = p.emit(
                                "tensor",
                                lambda e, P=P, buf=buf, j=j, c0=c0,
                                st=(op == 0 and scl == 0),
                                sp=(op == 6 and scl == 3): e.matmul(
                                    P[:, c0:c0 + 512],
                                    buf[:, j * 128:(j + 1) * 128],
                                    cop(j, c0, 512), start=st, stop=sp),
                                wait=w,
                                inc=[("MM", 1)] if (nt == NT - 1 and op == 6
                                                    and scl == 3) else ())
                grp[hc] = tt["MM"]
                if hc + 2 < KC:
                    emit_cload(hc + 2)
                w = [("MM", grp[hc]),
                     ("GS", gs_tok[hc - 1] if hc >= 1 else gs_prev)]
                if pz == 0 and hc == 0:
                    w.append(("GT", t_gt_final))
                t = p.emit("scalar",
                           lambda e, P=P: e.activation(stf[:, :], P[:, :],
                                                       AF.Copy),
                           wait=w, inc=[("AC", 1)])
                ac_ev[hc] = t["AC"]
                kw = {} if pz == 0 else {"accum_op": ALU.add}
                tt = dma("gpsimd", "GS", s_d[hc, :, :], stf[:, :],
                         wait=[("AC", ac_ev[hc])], **kw)
                gs_tok[hc] = tt["GS"]
            ac_prev_pass = ac_ev[KC - 1]
            grp_prev_pass = grp[KC - 1]

    if "e" in phases:
        SLD = 26624
        GLD = 28672
        s_ld = xf32[:, SLD:SLD + 2048]
        g_ld = xf32[:, GLD:GLD + 2048]
        ac_prev = p.val("AC")
        grp_prev_pass = p.val("MM")
        gs_e2_final = 0
        for tz in range(3):
            w_ops = [("MM", grp_prev_pass), ("AS0", t_as_final[0]),
                     ("AS1", t_as_final[1])]
            bnds = [0, 6, 11, 16, 21, 26]
            ol_tok = [0] * 5
            chunk_of = [0] * 26
            for ci in range(5):
                lo, hi = bnds[ci], bnds[ci + 1]
                for f in range(lo, hi):
                    chunk_of[f] = ci
                sem_o = "OL%d" % ci
                tt = dma("sync", sem_o,
                         xnbuf[:, lo * 2048:hi * 2048].rearrange(
                             "p (f t) -> p f t", f=hi - lo),
                         a_d[tz * 26 + lo:tz * 26 + hi].rearrange(
                             "f p t -> p f t"),
                         wait=w_ops)
                ol_tok[ci] = tt[sem_o]

            wtok = [0] * KC
            grp = [0] * KC
            ac_ev = [0] * KC
            gs_tok = [0] * KC
            ev_chain = [0] * KC
            os_out = [0] * KC
            gs_prev = p.val("GS")

            def emit_eload(hc, tz=tz, grp=grp, wtok=wtok, gpp=grp_prev_pass):
                w = []
                if hc >= 2:
                    w.append(("MM", grp[hc - 2]))
                else:
                    w.append(("MM", gpp))
                buf = [wbufA, wbufB][hc % 2]
                sem = ["WA", "WB"][hc % 2]
                tt = dma("gpsimd", sem,
                         buf[:, 0:3328].rearrange("p (f m) -> p f m", f=26),
                         wdb[tz, hc].rearrange("f p m -> p f m"), wait=w)
                wtok[hc] = tt[sem]

            emit_eload(0)
            emit_eload(1)
            for hc in range(KC):
                buf = [wbufA, wbufB][hc % 2]
                sem = ["WA", "WB"][hc % 2]
                P = PS[hc % 2]
                wb = [(sem, wtok[hc])]
                if tz < 2:
                    if hc >= 2:
                        wb.append(("AC", ac_ev[hc - 2]))
                    else:
                        wb.append(("AC", ac_prev))
                else:
                    if hc >= 2:
                        wb.append(("EV", ev_chain[hc - 2]))
                    else:
                        wb.append(("AC", ac_prev))
                for nt in range(NT):
                    c0 = nt * 512
                    for fsc in range(26):
                        w = wb if (nt == 0 and fsc == 0) else ()
                        if nt == 0 and (fsc == 0 or chunk_of[fsc] !=
                                        chunk_of[fsc - 1]):
                            w = list(w) + [("OL%d" % chunk_of[fsc],
                                            ol_tok[chunk_of[fsc]])]
                        tt = p.emit("tensor",
                                    lambda e, P=P, buf=buf, fsc=fsc, c0=c0:
                                    e.matmul(P[:, c0:c0 + 512],
                                             buf[:, fsc * 128:(fsc + 1) * 128],
                                             cop(fsc, c0, 512),
                                             start=(fsc == 0),
                                             stop=(fsc == 25)),
                                    wait=w,
                                    inc=[("MM", 1)] if (nt == NT - 1 and
                                                        fsc == 25) else ())
                grp[hc] = tt["MM"]
                if hc + 2 < KC:
                    emit_eload(hc + 2)

                if tz < 2:
                    w = [("MM", grp[hc]),
                         ("GS", gs_tok[hc - 1] if hc >= 1 else gs_prev)]
                    t = p.emit("scalar",
                               lambda e, P=P: e.activation(stf[:, :], P[:, :],
                                                           AF.Copy),
                               wait=w, inc=[("AC", 1)])
                    ac_ev[hc] = t["AC"]
                    tt = dma("gpsimd", "GS", s_d[hc, :, :], stf[:, :],
                             wait=[("AC", ac_ev[hc])], accum_op=ALU.add)
                    gs_tok[hc] = tt["GS"]
                else:
                    wl = [("GS", gs_e2_final)]
                    if hc >= 1:
                        wl.append(("EV", ev_chain[hc - 1]))
                    dma("sync", "EL", s_ld, s_d[hc, :, :], wait=wl)
                    dma("sync", "EL", g_ld, g_d[hc, :, :],
                        wait=[("GT", t_gt_final)])
                    wx = [("OS", os_out[hc - 1])] if hc >= 1 else \
                        [("GS", gs_e2_final)]
                    tt = dma("sync", "EL", stf[:, :],
                             x_fm[hc * 128:(hc + 1) * 128, :], wait=wx)
                    t_el = tt["EL"]
                    p.emit("vector",
                           lambda e, P=P: e.tensor_tensor(s_ld, P[:, :], s_ld,
                                                          ALU.add),
                           wait=[("MM", grp[hc]), ("EL", t_el)])
                    p.emit("vector",
                           lambda e: e.tensor_tensor(g_ld, s_ld, g_ld,
                                                     ALU.mult))
                    tt = p.emit("vector",
                                lambda e: e.tensor_tensor(stf[:, :], g_ld,
                                                          stf[:, :], ALU.add),
                                inc=[("EV", 1)])
                    ev_chain[hc] = tt["EV"]
                    tt = dma("sync", "OS", out_d[hc, :, :], stf[:, :],
                             wait=[("EV", ev_chain[hc])])
                    os_out[hc] = tt["OS"]
            if tz == 1:
                gs_e2_final = p.val("GS")
            if tz < 2:
                ac_prev = ac_ev[KC - 1]
            grp_prev_pass = grp[KC - 1]

    fin_sync = [(s, p.val(s)) for s in
                ("OS", "GT", "TS0", "TS1", "TS2", "TS3", "AS0", "AS1",
                 "OL0", "OL1", "OL2", "OL3", "OL4", "OL5", "OL6",
                 "EL", "CLS", "DBG")]
    p.emit("sync", lambda e: e.nop(), wait=fin_sync)
    fin_gp = [(s, p.val(s)) for s in
              ("GS", "CLG", "WA", "WB", "X0", "X1")]
    p.emit("gpsimd", lambda e: e.nop(), wait=fin_gp)

    from contextlib import ExitStack
    with ExitStack() as ctx:
        for name in SEMS:
            sem_handles[name] = ctx.enter_context(nc.semaphore(name))
        with nc.Block() as block:
            for eng in Prog.ENGINES:
                if p.items[eng]:
                    getattr(block, eng)(_emitter(p.items[eng], sem_handles))
    return nc


def host_prep(inputs):
    f4 = np.float32
    up = np.asarray(inputs["L0_attn_up_w"], f4)
    ga = np.asarray(inputs["L0_attn_gate_w"], f4)
    lp = np.asarray(inputs["L0_attn_local_point"], f4)
    lao = np.asarray(inputs["L0_attn_local_approx_output"], f4)
    fw = np.asarray(inputs["L0_attn_fuse_weight"], f4)
    df = np.asarray(inputs["L0_attn_discount_factor"], f4)
    wg = np.asarray(inputs["L0_attn_ffn_gate_w"], f4)
    wu = np.asarray(inputs["L0_attn_ffn_up_w"], f4)
    wd = np.asarray(inputs["L0_attn_ffn_down_w"], f4)
    wbase = np.asarray(inputs["L0_ffn_W_base"], f4)
    bbase = np.asarray(inputs["L0_ffn_b_base"], f4)
    wgate = np.asarray(inputs["L0_ffn_W_gate"], f4)
    wthr = np.asarray(inputs["L0_ffn_W_thr"], f4)
    bthr = np.asarray(inputs["L0_ffn_b_thr"], f4)
    Am = np.asarray(inputs["L0_ffn_A"], f4)
    Bm = np.asarray(inputs["L0_ffn_B"], f4)

    shared = {}
    shared["upb"] = np.ascontiguousarray(
        up.T.reshape(KC, 128, SCN, 128).transpose(2, 0, 1, 3))
    shared["gab"] = np.ascontiguousarray(
        ga.T.reshape(KC, 128, SCN, 128).transpose(2, 0, 1, 3))
    W7 = np.concatenate([lao.T[None], fw.transpose(2, 1, 0)], 0)
    shared["w7b"] = np.ascontiguousarray(
        W7.reshape(7, 2, 4, 128, KC, 128).transpose(1, 4, 0, 2, 3, 5))
    shared["wgb"] = np.ascontiguousarray(
        wg.T.reshape(KC, 128, FCN, 128).transpose(2, 0, 1, 3))
    shared["wub"] = np.ascontiguousarray(
        wu.T.reshape(KC, 128, FCN, 128).transpose(2, 0, 1, 3))
    shared["wdb"] = np.ascontiguousarray(
        wd.T.reshape(3, 26, 128, KC, 128).transpose(0, 3, 1, 2, 4))
    shared["wbb"] = np.ascontiguousarray(
        wbase.T.reshape(KC, 128, KC, 128).transpose(2, 0, 1, 3))
    shared["aab"] = np.ascontiguousarray(
        Am.transpose(2, 0, 1).reshape(H, 128).reshape(KC, 128, 128))
    shared["btb"] = np.ascontiguousarray(Bm.transpose(0, 2, 1).reshape(128, H))
    w9 = np.concatenate([wgate, wthr], 0)
    shared["w9b"] = np.ascontiguousarray(
        w9.T.reshape(KC, 128, 9).transpose(1, 0, 2).reshape(128, 288))
    shared["lpb"] = np.ascontiguousarray(lp.reshape(8, 128).T)
    shared["bbb"] = np.ascontiguousarray(bbase.reshape(KC, 128).T)
    shared["bthr"] = np.ascontiguousarray(bthr.reshape(1, 1))
    shared["dfac"] = np.ascontiguousarray(df.reshape(1, 6))
    shared["onesk"] = np.ones((128, 1), f4)
    shared["ones1"] = np.ones((1, 128), f4)
    shared["identf"] = np.eye(128, dtype=f4)
    ind = np.zeros((8, 128), f4)
    for e in range(8):
        ind[e, e * 16:(e + 1) * 16] = 2.0
    shared["indw"] = ind
    shared["cvals"] = np.array([[EPS, 1.0 / H, -1.0, 0.0]], f4)

    x = np.asarray(inputs["L0_attn_hidden_states"], f4).reshape(16384, H)
    xfms = [np.ascontiguousarray(x[c * NTOK:(c + 1) * NTOK].T)
            for c in range(NCORE)]
    return shared, xfms


def kernel(**inputs):
    shared, xfms = host_prep(inputs)
    nc = build()
    in_maps = [dict(shared, x_fm=xfms[c]) for c in range(NCORE)]
    res = run_bass_kernel_spmd(nc, in_maps, core_ids=list(range(NCORE)))
    out = np.empty((16384, H), np.float32)
    for c in range(NCORE):
        oc = np.asarray(res.results[c]["out"], np.float32).reshape(H, NTOK)
        out[c * NTOK:(c + 1) * NTOK] = oc.T
    return out.reshape(4, 4096, 4096)
